# version 03
# baseline (speedup 1.0000x reference)
import numpy as np
import concourse.bass as bass
import concourse.mybir as mybir
from concourse.bass_utils import run_bass_kernel_spmd

F32 = mybir.dt.float32
BF16 = mybir.dt.bfloat16
I32 = mybir.dt.int32
U8 = mybir.dt.uint8
AF = mybir.ActivationFunctionType
ALU = mybir.AluOpType
AX = mybir.AxisListType

B, L, D = 4, 8192, 1024
T = 4096
NCH = 8
NH0, DK0, U0 = 16, 64, 50
PAD, HALO = 512, 1152
LT = PAD + HALO + T
NT_L = LT // 128
FH, FDK = 8, 64
INNER = 512
HID, HIDP = 2730, 2816
NEG = -1.0e30
OWN0 = PAD + HALO

ENGINES = ("sync", "vector", "scalar", "gpsimd", "tensor")


class Sem:
    def __init__(self, handle):
        self.h = handle
        self.v = 0

    def inc(self, n=1):
        self.v += n
        return self.v


class Prog:
    def __init__(self, nc):
        self.nc = nc
        self.ops = {e: [] for e in ENGINES}
        self.sems = {}
        self.sb = {}
        self.ps = {}
        self.dram = {}
        self._guards = []
        self.last = {e: None for e in ENGINES}
        self.last_any = {e: None for e in ENGINES}
        self.chained = {"vector", "scalar", "gpsimd"}

    def sem(self, name):
        if name not in self.sems:
            g = self.nc.semaphore(name)
            h = g.__enter__()
            self._guards.append(g)
            self.sems[name] = Sem(h)
        return self.sems[name]

    def sbuf(self, name, shape, dtype, tmp=False):
        key = name
        if key not in self.sb:
            self._namegen = getattr(self, "_namegen", {})
            gen = self._namegen.get(name, 0)
            self._namegen[name] = gen + 1
            uname = name if gen == 0 else f"{name}__g{gen}"
            g = self.nc.sbuf_tensor(uname, shape, dtype,
                                    side="right" if tmp else "left")
            h = g.__enter__()
            self.sb[key] = h
            self._sb_guards = getattr(self, "_sb_guards", {})
            self._sb_guards[key] = g
            if tmp:
                self._tmp = getattr(self, "_tmp", [])
                self._tmp.append(key)
            else:
                self._left_order = getattr(self, "_left_order", [])
                self._left_order.append(key)
        return self.sb[key]

    def free_left_after(self, marker):
        order = getattr(self, "_left_order", [])
        i = order.index(marker)
        for key in reversed(order[i + 1:]):
            self.sbuf_free(key)
        self._left_order = order[:i + 1]

    def sbuf_free(self, *names):
        for key in names:
            if key in self.sb:
                self._sb_guards[key].__exit__(None, None, None)
                del self.sb[key]
                del self._sb_guards[key]

    def free_tmps(self):
        for key in reversed(list(getattr(self, "_tmp", []))):
            self.sbuf_free(key)
        self._tmp = []

    def psum(self, name, shape, dtype=F32):
        if name not in self.ps:
            g = self.nc.psum_tensor(name, shape, dtype)
            self.ps[name] = g.__enter__()
            self._guards.append(g)
        return self.ps[name]

    def on(self, eng, fn):
        self.ops[eng].append(fn)

    def _chain_wait(self, eng):
        lw = self.last[eng]
        if eng in self.chained and lw is not None:
            s, v = lw
            self.on(eng, lambda e, s=s, v=v: e.wait_ge(s, v))
            self.last[eng] = None

    def dma(self, eng, out, in_, sem, n=16, **kw):
        self._chain_wait(eng)
        tgt = sem.inc(n)
        self.on(eng, lambda e, out=out, in_=in_, s=sem.h, kw=kw:
                e.dma_start(out=out, in_=in_, **kw).then_inc(s, n))

    def wait(self, eng, sem, val=None):
        v = sem.v if val is None else val
        self.on(eng, lambda e, s=sem.h, v=v: e.wait_ge(s, v))

    def run(self, eng, fn, sem=None, n=1, chain=True):
        if chain:
            self._chain_wait(eng)
        if sem is None and eng in self.chained:
            sem = self.sem("_auto_" + eng)
        if sem is None:
            self.on(eng, fn)
        else:
            tgt = sem.inc(n)
            self.on(eng, lambda e, fn=fn, s=sem.h, n=n: fn(e).then_inc(s, n))
            self.last[eng] = (sem.h, tgt)
            self.last_any[eng] = (sem.h, tgt)

    def runf(self, eng, fn):
        self.run(eng, fn)

    def fence(self, eng):
        self._chain_wait(eng)

    def mark(self, eng):
        if self.last_any[eng] is not None:
            return self.last_any[eng]
        a = self.sem("_auto_" + eng)
        return (a.h, a.v)

    def wait_mark(self, eng, mk):
        s, v = mk
        self.on(eng, lambda e, s=s, v=v: e.wait_ge(s, v))

    def PS(self, i):
        return self.psum(f"_ps{i}", [128, 512], F32)

    def PSB(self, i):
        return self.psum(f"_psb{i}", [128, 512], BF16)

    def barrier(self, name="bar"):
        bs = self.sem("_barrier")
        for e in ENGINES:
            self._chain_wait(e)
            self.on(e, lambda en, s=bs.h: en.nop().then_inc(s, 1))
        bs.v += 5
        tgt = bs.v
        for e in ENGINES:
            self.on(e, lambda en, s=bs.h, t=tgt: en.wait_ge(s, t))

    def build_block(self):
        nc = self.nc
        with nc.Block() as block:
            @block.sync
            def _(q):
                for f in self.ops["sync"]:
                    f(q)

            @block.vector
            def _(v):
                for f in self.ops["vector"]:
                    f(v)

            @block.scalar
            def _(s):
                for f in self.ops["scalar"]:
                    f(s)

            @block.gpsimd
            def _(g):
                for f in self.ops["gpsimd"]:
                    f(g)

            @block.tensor
            def _(t):
                for f in self.ops["tensor"]:
                    f(t)



def prep_inputs(inp):
    cm = {}
    cm["WqT"] = np.ascontiguousarray(inp["L0_attn_Wq"].T)
    cm["WkT"] = np.ascontiguousarray(inp["L0_attn_Wk"].T)
    cm["WvT"] = np.ascontiguousarray(inp["L0_attn_Wv"].T)
    cm["WoT"] = np.ascontiguousarray(inp["L0_attn_Wo"].T)
    for nm in ("bq", "bk", "bv", "bo"):
        cm["a_" + nm] = np.ascontiguousarray(
            np.asarray(inp["L0_attn_" + nm]).reshape(NCH, 128))
    cm["idx_k"] = np.ascontiguousarray(
        np.asarray(inp["L0_attn_idx_k"]).reshape(U0, 1).astype(np.int32))

    w1 = inp["L0_ffn_w1"]; b1 = inp["L0_ffn_b1"]; w2 = inp["L0_ffn_w2"]
    for li in range(2):
        cm[f"f{li}_wqkv"] = np.ascontiguousarray(inp["L0_ffn_wqkv"][li])
        cm[f"f{li}_wo"] = np.ascontiguousarray(inp["L0_ffn_wo"][li])
        cm[f"f{li}_qs"] = np.ascontiguousarray(
            np.asarray(inp["L0_ffn_q_scale"][li]).reshape(1, FDK))
        cm[f"f{li}_ks"] = np.ascontiguousarray(
            np.asarray(inp["L0_ffn_k_scale"][li]).reshape(1, FDK))
        cm[f"f{li}_lng"] = np.ascontiguousarray(
            np.asarray(inp["L0_ffn_ln_g"][li]).reshape(NCH, 128))
        cm[f"f{li}_lnb"] = np.ascontiguousarray(
            np.asarray(inp["L0_ffn_ln_b"][li]).reshape(NCH, 128))
        w1a = np.zeros((D, HIDP), np.float32); w1a[:, :HID] = w1[li][:, :HID]
        w1g = np.zeros((D, HIDP), np.float32); w1g[:, :HID] = w1[li][:, HID:]
        cm[f"f{li}_w1a"] = w1a
        cm[f"f{li}_w1g"] = w1g
        b1a = np.zeros((HIDP // 128, 128), np.float32)
        b1a.reshape(-1)[:HID] = np.asarray(b1[li])[:HID]
        b1g = np.zeros((HIDP // 128, 128), np.float32)
        b1g.reshape(-1)[:HID] = np.asarray(b1[li])[HID:]
        cm[f"f{li}_b1a"] = b1a
        cm[f"f{li}_b1g"] = b1g
        w2p = np.zeros((HIDP, D), np.float32); w2p[:HID] = w2[li]
        cm[f"f{li}_w2"] = w2p
        cm[f"f{li}_b2"] = np.ascontiguousarray(
            np.asarray(inp["L0_ffn_b2"][li]).reshape(NCH, 128))

    cm["g_aparam"] = np.ascontiguousarray(
        np.asarray(inp["L1_attn_a_param"]).reshape(NCH, 128))
    cm["g_win"] = np.ascontiguousarray(inp["L1_attn_w_in"])
    cm["g_bin"] = np.ascontiguousarray(inp["L1_attn_b_in"])
    cm["g_wa"] = np.ascontiguousarray(inp["L1_attn_w_a"])
    cm["g_ba"] = np.ascontiguousarray(inp["L1_attn_b_a"])
    cm["m_routerT"] = np.ascontiguousarray(np.asarray(inp["L1_ffn_router_w"]).T)
    cm["m_ebias"] = np.ascontiguousarray(
        np.asarray(inp["L1_ffn_experts_bias"]).reshape(1, 8))
    cm["m_w1"] = np.ascontiguousarray(
        np.concatenate([inp["L1_ffn_w1"], inp["L1_ffn_sw1"]], 0))
    cm["m_w3"] = np.ascontiguousarray(
        np.concatenate([inp["L1_ffn_w3"], inp["L1_ffn_sw3"]], 0))
    cm["m_w2"] = np.ascontiguousarray(
        np.concatenate([inp["L1_ffn_w2"], inp["L1_ffn_sw2"]], 0))

    cm["ident"] = np.eye(128, dtype=np.float32)
    cm["ones128"] = np.ones((128, 128), np.float32)
    cm["iota50"] = np.broadcast_to(np.arange(U0, dtype=np.float32), (128, U0)).copy()
    cm["iota8"] = np.broadcast_to(np.arange(8, dtype=np.float32), (128, 8)).copy()
    cm["iota_seg"] = (np.arange(128)[:, None] // 16 * 1024
                      + np.arange(1024)[None, :]).astype(np.float32)
    r = np.arange(128)[:, None]; c = np.arange(640)[None, :]
    cm["band"] = np.where((c - r >= 0) & (c - r <= 512), 0.0, NEG).astype(np.float32)

    half = np.arange(0, FDK, 2, dtype=np.float64)
    inv_freq = 1.0 / 10000.0 ** (half / FDK)
    tpos = np.arange(L, dtype=np.float64)
    freqs = np.concatenate([tpos[:, None] * inv_freq[None, :]] * 2, 1)
    base = (half + 0.4 * FDK) / (1.4 * FDK)
    power = (tpos - L // 2) / 256.0
    scale = np.concatenate([base[None, :] ** power[:, None]] * 2, 1)
    cos2, sin2 = np.cos(freqs), np.sin(freqs)
    sgn = np.concatenate([-np.ones(32), np.ones(32)])[None, :]
    QC = (cos2 * scale).astype(np.float32)
    QS = (sin2 * scale * sgn).astype(np.float32)
    KC = (cos2 / scale).astype(np.float32)
    KS = (sin2 / scale * sgn).astype(np.float32)

    per_core = []
    x = np.asarray(inp["L0_attn_x"])
    for core in range(8):
        b, hf = core // 2, core % 2
        m = {}
        m["x_b"] = np.ascontiguousarray(x[b])
        own0 = hf * T
        xl = np.zeros((HALO + T, D), np.float32)
        if hf == 0:
            xl[HALO:] = x[b, 0:T]
        else:
            xl[:] = x[b, T - HALO:2 * T]
        m["x_loc"] = xl
        val = np.full((1, LT), NEG, np.float32)
        val[0, (OWN0 if hf == 0 else PAD):] = 0.0
        m["validity"] = val
        g0 = own0 - HALO - PAD
        gidx = np.clip(np.arange(LT) + g0, 0, L - 1)
        m["rot_qc"] = QC[gidx]; m["rot_qs"] = QS[gidx]
        m["rot_kc"] = KC[gidx]; m["rot_ks"] = KS[gidx]
        m["half_off"] = np.full((128, 1), float(own0), np.float32)
        m["keep_col"] = np.full((128, 1), 0.0 if hf == 0 else 1.0, np.float32)
        m["carry_use"] = np.full((128, 1), float(hf), np.float32)
        per_core.append(m)
    return cm, per_core


INPUT_SPECS = {
    "x_b": ([L, D], F32), "x_loc": ([HALO + T, D], F32),
    "validity": ([1, LT], F32),
    "rot_qc": ([LT, FDK], F32), "rot_qs": ([LT, FDK], F32),
    "rot_kc": ([LT, FDK], F32), "rot_ks": ([LT, FDK], F32),
    "half_off": ([128, 1], F32), "keep_col": ([128, 1], F32),
    "carry_use": ([128, 1], F32),
    "WqT": ([D, D], F32), "WkT": ([D, D], F32), "WvT": ([D, D], F32),
    "WoT": ([D, D], F32),
    "a_bq": ([NCH, 128], F32), "a_bk": ([NCH, 128], F32),
    "a_bv": ([NCH, 128], F32), "a_bo": ([NCH, 128], F32),
    "idx_k": ([U0, 1], I32),
    "g_aparam": ([NCH, 128], F32),
    "g_win": ([8, 128, 128], F32), "g_bin": ([8, 128], F32),
    "g_wa": ([8, 128, 128], F32), "g_ba": ([8, 128], F32),
    "m_routerT": ([D, 8], F32), "m_ebias": ([1, 8], F32),
    "m_w1": ([9, D, 2048], F32), "m_w3": ([9, D, 2048], F32),
    "m_w2": ([9, 2048, D], F32),
    "ident": ([128, 128], F32), "ones128": ([128, 128], F32),
    "iota50": ([128, U0], F32), "iota8": ([128, 8], F32),
    "iota_seg": ([128, 1024], F32), "band": ([128, 640], F32),
}
for _li in range(2):
    INPUT_SPECS.update({
        f"f{_li}_wqkv": ([D, 3 * INNER], F32), f"f{_li}_wo": ([INNER, D], F32),
        f"f{_li}_qs": ([1, FDK], F32), f"f{_li}_ks": ([1, FDK], F32),
        f"f{_li}_lng": ([NCH, 128], F32), f"f{_li}_lnb": ([NCH, 128], F32),
        f"f{_li}_w1a": ([D, HIDP], F32), f"f{_li}_w1g": ([D, HIDP], F32),
        f"f{_li}_b1a": ([HIDP // 128, 128], F32), f"f{_li}_b1g": ([HIDP // 128, 128], F32),
        f"f{_li}_w2": ([HIDP, D], F32), f"f{_li}_b2": ([NCH, 128], F32),
    })

SCRATCH_SPECS = {
    "xnf_b": ([NCH, 128, L], BF16),
    "xnf_l": ([NCH, 128, LT], BF16),
    "qf_b": ([NCH, 128, L], BF16),
    "kfa_b": ([NH0, 65, L], BF16),
    "vtok_b": ([L, NH0 * 65], BF16),
    "attnout": ([T + 8, D], F32),
    "xmid_f": ([NCH, 128, LT], F32),
    "xmid_b": ([NCH, 128, LT], BF16),
    "xout_f": ([2, NCH, 128, LT], F32),
    "xout_b": ([2, NCH, 128, LT], BF16),
    "qaf_l": ([FH, 65, LT], BF16),
    "kaf_l": ([FH, 65, LT], BF16),
    "vtok_l": ([LT, FH * FDK], BF16),
    "of_l": ([INNER, LT], BF16),
    "x1f": ([NCH, 128, T], F32),
    "xn1f": ([NCH, 128, T], F32),
    "xn1b": ([NCH, 128, T], BF16),
    "hscan": ([NCH, 128, T], F32),
    "carry_out": ([1, 4096], F32),
    "carry_gath": ([2, 4096], F32),
    "ppl": ([NCH, 128, T + 1], F32),
    "normed": ([NCH, 128, T], F32),
    "cwT": ([8, T], F32),
    "moeacc": ([NCH, 128, T], F32),
    "out_own": ([T, D], F32),
}



def emit_p0(p: Prog):
    d = p.dram
    ident = p.sbuf("identb", [128, 128], BF16)
    xt = p.sbuf("p0_x", [128, 2 * D], F32, tmp=True)
    xnb = p.sbuf("p0_xn", [128, 2 * D], BF16, tmp=True)
    junk = p.sbuf("p0_junk", [128, D], F32, tmp=True)
    stat = p.sbuf("p0_stat", [128, 8], F32, tmp=True)
    pst = [p.PSB(0)[:, 0:128], p.PSB(1)[:, 0:128]]
    ev = p.sbuf("p0_ev", [128, 2 * D], BF16, tmp=True)

    ld = p.sem("p0_ld"); vs = p.sem("p0_vs"); ss = p.sem("p0_ss")
    tp = p.sem("p0_tp"); evs = p.sem("p0_ev"); st = p.sem("p0_st")

    p.dma("sync", ident[:], d["ident"], ld)
    p.ops["sync"].pop()
    ld.v -= 16
    p.dma("gpsimd", ident[:], d["ident"], ld)

    jobs = []
    for i in range(L // 128):
        jobs.append(("x_b", i * 128, "xnf_b", i * 128, L))
    for i in range(PAD // 128, NT_L):
        jobs.append(("x_loc", i * 128 - PAD, "xnf_l", i * 128, LT))
    zpad = p.sbuf("p0_zpad", [128, PAD], BF16, tmp=True)
    p.run("vector", lambda e: e.memset(zpad[:], 0.0), vs)
    for c in range(NCH):
        p.wait("sync", vs)
        p.dma("sync", d["xnf_l"].rearrange("c p n -> p c n")[:, c, 0:PAD], zpad[:], st)

    for j, (srcn, r0, dstn, c0, ntok) in enumerate(jobs):
        pb = j % 2
        x = xt[:, pb * D:(pb + 1) * D]
        xn = xnb[:, pb * D:(pb + 1) * D]
        evt = ev[:, pb * D:(pb + 1) * D]
        mu = stat[:, 4 * pb:4 * pb + 1]
        nmu = stat[:, 4 * pb + 1:4 * pb + 2]
        var = stat[:, 4 * pb + 2:4 * pb + 3]
        rstd = stat[:, 4 * pb + 3:4 * pb + 4]
        if j >= 2:
            p.wait("sync", ss, 3 * (j - 1))
        p.dma("sync", x, d[srcn][r0:r0 + 128, :], ld)
        p.wait("vector", ld)
        if j >= 2:
            p.wait("vector", ss, 3 * (j - 1))
        p.run("vector", lambda e, x=x, mu=mu: e.reduce_sum(mu, x, AX.X))
        p.run("vector", lambda e, mu=mu: e.tensor_scalar_mul(mu, mu, 1.0 / D))
        p.run("vector", lambda e, mu=mu, nmu=nmu:
              e.tensor_scalar(nmu, mu, -1.0, None, ALU.mult), vs)
        p.wait("scalar", vs)
        p.run("scalar", lambda e, x=x, nmu=nmu, var=var:
              e.activation(junk[:], x, AF.Square, bias=nmu, scale=1.0,
                           accum_out=var), ss)
        p.wait("vector", ss)
        p.runf("vector", lambda e, var=var:
               e.tensor_scalar(var, var, 1.0 / D, 1e-5, ALU.mult, ALU.add))
        p.fence("vector")
        p.run("vector", lambda e, var=var, rstd=rstd: e.reciprocal(rstd, var), vs)
        p.wait("scalar", vs)
        p.run("scalar", lambda e, rstd=rstd: e.sqrt(rstd, rstd))
        if j >= 2:
            p.wait("scalar", tp, 8 * (j - 1))
        p.run("scalar", lambda e, x=x, xn=xn, nmu=nmu:
              e.activation(xn, x, AF.Identity, bias=nmu, scale=1.0))
        p.run("scalar", lambda e, xn=xn, rstd=rstd: e.mul(xn, xn, rstd), ss, 2)
        p.wait("tensor", ss)
        for c in range(NCH):
            bank = c % 2
            if j * NCH + c >= 2:
                p.wait("tensor", evs, j * NCH + c - 1)
            p.run("tensor", lambda e, xn=xn, c=c, bank=bank:
                  e.transpose(pst[bank], xn[:, c * 128:(c + 1) * 128], ident[:]),
                  tp)
            p.wait("vector", tp)
            if j >= 2 and c == 0:
                p.wait("vector", st)
            p.run("vector", lambda e, evt=evt, c=c, bank=bank:
                  e.tensor_copy(evt[:, c * 128:(c + 1) * 128], pst[bank]), evs)
        p.wait("scalar", evs)
        dst = d[dstn].rearrange("c p n -> p c n")[:, :, c0:c0 + 128]
        p.dma("scalar", dst, evt.rearrange("p (c n) -> p c n", c=NCH), st)
    p.wait("sync", st)
    p.barrier()
    p.free_tmps()



def build_program(phases, external_scratch=(), core_inputs=None):
    nc = bass.Bass("TRN2", target_bir_lowering=False)
    p = Prog(nc)
    used = core_inputs if core_inputs is not None else INPUT_SPECS.keys()
    for name in used:
        shape, dt = INPUT_SPECS[name]
        p.dram[name] = nc.dram_tensor(name, shape, dt, kind="ExternalInput").ap()
    ext = {}
    for s in external_scratch:
        if s.startswith("in:"):
            ext[s[3:]] = "ExternalInput"
        else:
            ext[s] = "ExternalOutput"
    for name, (shape, dt) in SCRATCH_SPECS.items():
        kind = ext.get(name, "Internal")
        p.dram[name] = nc.dram_tensor(name, shape, dt, kind=kind).ap()
    for ph in phases:
        ph(p)
    p.build_block()
    return nc, p


PH1 = None
PH2 = None


def _phases():
    global PH1, PH2
    from functools import partial
    PH1 = [emit_common, emit_p0, emit_p1_weights, emit_p1a, emit_p1b,
           emit_p1c, emit_p1de, emit_p2, emit_p3, emit_p5a]
    PH2 = [emit_common, emit_p5b, emit_p6a, emit_p6, emit_p7]


def kernel(**inputs):
    _phases()
    cm, per_core = prep_inputs(inputs)
    handover = ["ppl", "normed", "x1f", "xn1f", "xn1b"]
    nc1, _ = build_program(PH1, external_scratch=["carry_out"] + handover)
    in_maps1 = [dict(cm, **per_core[c]) for c in range(8)]
    res1 = run_bass_kernel_spmd(nc1, in_maps1, core_ids=list(range(8)))
    in_maps2 = []
    for c in range(8):
        m = dict(cm, **per_core[c])
        for k in handover:
            m[k] = res1.results[c][k]
        cg = np.zeros((2, 4096), np.float32)
        if c % 2 == 1:
            cg[0] = res1.results[c - 1]["carry_out"][0]
        m["carry_gath"] = cg
        in_maps2.append(m)
    nc2, _ = build_program(PH2, external_scratch=["in:" + k for k in handover]
                           + ["in:carry_gath", "out_own"])
    res2 = run_bass_kernel_spmd(nc2, in_maps2, core_ids=list(range(8)))
    out = np.zeros((B, L, D), np.float32)
    for c in range(8):
        out[c // 2, (c % 2) * T:(c % 2 + 1) * T] = res2.results[c]["out_own"]
    return out




def emit_common(p: Prog):
    d = p.dram
    ld = p.sem("cm_ld")
    identb = p.sbuf("identb", [128, 128], BF16)
    ident32 = p.sbuf("ident32", [128, 128], F32)
    onesr = p.sbuf("ones_1x128", [1, 128], F32)
    p.dma("gpsimd", identb[:], d["ident"], ld)
    p.dma("sync", ident32[:], d["ident"], ld)
    p.run("vector", lambda e: e.memset(onesr[:], 1.0))
    p.barrier()



def emit_p1_weights(p: Prog):
    d = p.dram
    wld = p.sem("p1_wld")
    for nm in ("WqT", "WkT", "WvT", "WoT"):
        sb = p.sbuf(nm + "_sb", [128, NCH * 1024], BF16)
        for c in range(NCH):
            p.dma("gpsimd", sb[:, c * 1024:(c + 1) * 1024],
                  d[nm][c * 128:(c + 1) * 128, :], wld)
    for nm in ("a_bq", "a_bk", "a_bv", "a_bo"):
        sb = p.sbuf(nm + "_sb", [128, NCH], F32)
        p.dma("sync", sb[:], d[nm].rearrange("c p -> p c"), wld,
              allow_slow_non_contiguous=True)
    bo_row = p.sbuf("bo_row", [1, D], F32)
    p.dma("sync", bo_row[:], d["a_bo"].rearrange("c p -> (c p)").rearrange("(a n) -> a n", a=1), wld)
    onesr = p.sb["ones_1x128"]
    bvr = p.sbuf("p1_bvr1", [1, D], F32)
    psb = [p.PS(0), p.PS(1)]
    bvrow = p.sbuf("bvrow", [128, D], F32)
    mm = p.sem("p1w_mm")
    p.dma("sync", bvr[:], d["a_bv"].rearrange("c p -> (c p)").rearrange("(a n) -> a n", a=1), wld)
    p.wait("tensor", wld)
    p.wait_mark("tensor", p.mark("vector"))
    for half in range(2):
        p.run("tensor", lambda e, half=half:
              e.matmul(psb[half][:], onesr[:], bvr[:, half * 512:(half + 1) * 512],
                       start=True, stop=True), mm)
    p.wait("vector", mm)
    for half in range(2):
        p.run("vector", lambda e, half=half:
              e.tensor_copy(bvrow[:, half * 512:(half + 1) * 512], psb[half][:]))
    p.barrier()


def emit_p1a(p: Prog):
    d = p.dram
    wq = p.sb["WqT_sb"]; wk = p.sb["WkT_sb"]; wv = p.sb["WvT_sb"]
    bq = p.sb["a_bq_sb"]; bk = p.sb["a_bk_sb"]
    bvrow = p.sb["bvrow"]
    xn = p.sbuf("p1a_xn", [128, NCH * 512], BF16, tmp=True)
    qev = p.sbuf("p1a_qev", [128, 2 * 512], BF16, tmp=True)
    kev = p.sbuf("p1a_kev", [128, 2 * 512], BF16, tmp=True)
    vasm = p.sbuf("p1a_vasm", [128, NH0 * 65], BF16, tmp=True)
    ones_row = p.sbuf("p1a_ones", [1, L], BF16, tmp=True)
    kabs = p.sbuf("p1a_kabs", [128, 2], F32, tmp=True)
    psq = p.PS(0)
    psk = p.PS(1)
    psv = [p.PS(2), p.PS(3)]

    ld = p.sem("p1a_ld"); mm = p.sem("p1a_mm"); st = p.sem("p1a_st")

    p.run("vector", lambda e: e.memset(ones_row[:], 1.0))
    p.run("vector", lambda e: e.memset(kabs[:], 0.0))
    p.run("vector", lambda e: e.memset(vasm[:], 0.0))
    p.run("vector", lambda e: e.memset(
        vasm.rearrange("p (h c) -> p h c", c=65)[:, :, 64:65], 1.0))
    p.wait_mark("sync", p.mark("vector"))
    for h in range(NH0):
        p.dma("sync", d["kfa_b"][h, 64:65, :], ones_row[:], st)

    NLG = L // 512
    q_free = k_free = v_free = None
    xn_busy = None
    qst_mark = [None, None]
    kst_mark = [None, None]
    for lg in range(NLG):
        if xn_busy is not None:
            p.wait_mark("sync", xn_busy)
        for c in range(NCH):
            p.dma("sync", xn[:, c * 512:(c + 1) * 512],
                  d["xnf_b"][c, :, lg * 512:(lg + 1) * 512], ld)
        p.wait("tensor", ld)
        for ic in range(NCH):
            if q_free is not None:
                p.wait_mark("tensor", q_free)
            for dc in range(NCH):
                p.run("tensor", lambda e, dc=dc, ic=ic:
                      e.matmul(psq[:], wq[:, dc * 1024 + ic * 128: dc * 1024 + (ic + 1) * 128],
                               xn[:, dc * 512:(dc + 1) * 512],
                               start=(dc == 0), stop=(dc == NCH - 1)),
                      mm if dc == NCH - 1 else None)
            qmm = p.mark("tensor")
            if k_free is not None:
                p.wait_mark("tensor", k_free)
            for dc in range(NCH):
                p.run("tensor", lambda e, dc=dc, ic=ic:
                      e.matmul(psk[:], wk[:, dc * 1024 + ic * 128: dc * 1024 + (ic + 1) * 128],
                               xn[:, dc * 512:(dc + 1) * 512],
                               start=(dc == 0), stop=(dc == NCH - 1)),
                      mm if dc == NCH - 1 else None)
            kmm = p.mark("tensor")
            pb = (lg * NCH + ic) % 2
            qv = qev[:, pb * 512:(pb + 1) * 512]
            kv = kev[:, pb * 512:(pb + 1) * 512]
            p.wait_mark("scalar", qmm)
            if qst_mark[pb] is not None:
                p.wait("scalar", st)
            p.run("scalar", lambda e, ic=ic, qv=qv:
                  e.activation(qv, psq[:], AF.Identity,
                               bias=bq[:, ic:ic + 1], scale=1.0))
            q_free = p.mark("scalar")
            p.dma("scalar", d["qf_b"][ic, :, lg * 512:(lg + 1) * 512], qv, st)
            qst_mark[pb] = True
            p.wait_mark("vector", kmm)
            if kst_mark[pb] is not None:
                p.wait("vector", st)
            p.run("vector", lambda e, ic=ic, kv=kv:
                  e.tensor_scalar(kv, psk[:], bk[:, ic:ic + 1], None, ALU.add))
            k_free = p.mark("vector")
            p.run("vector", lambda e, kv=kv: e.tensor_reduce(
                kabs[:, 1:2], kv, AX.X, ALU.max, apply_absolute_value=True))
            p.run("vector", lambda e: e.tensor_tensor(
                kabs[:, 0:1], kabs[:, 0:1], kabs[:, 1:2], ALU.max))
            kev_done = p.mark("vector")
            p.wait_mark("gpsimd", kev_done)
            for hh in range(2):
                h = 2 * ic + hh
                p.dma("gpsimd", d["kfa_b"][h, 0:64, lg * 512:(lg + 1) * 512],
                      kv[hh * 64:(hh + 1) * 64, :], st)
            kst_mark[pb] = True
        for ls in range(4):
            for half in range(2):
                if v_free is not None:
                    p.wait_mark("tensor", v_free if half == 1 or ls > 0 or lg > 0 else v_free)
                for dc in range(NCH):
                    p.run("tensor", lambda e, dc=dc, ls=ls, half=half:
                          e.matmul(psv[half][:],
                                   xn[:, dc * 512 + ls * 128: dc * 512 + (ls + 1) * 128],
                                   wv[:, dc * 1024 + half * 512: dc * 1024 + (half + 1) * 512],
                                   start=(dc == 0), stop=(dc == NCH - 1)),
                          mm if dc == NCH - 1 else None)
            vmm = p.mark("tensor")
            xn_busy = vmm if (ls == 3 and True) else xn_busy
            p.wait_mark("vector", vmm)
            for half in range(2):
                p.run("vector", lambda e, half=half:
                      e.tensor_tensor(
                          vasm.rearrange("p (h c) -> p h c", c=65)[:, half * 8:(half + 1) * 8, 0:64],
                          psv[half][:].rearrange("p (h c) -> p h c", c=64),
                          bvrow[:, half * 512:(half + 1) * 512].rearrange("p (h c) -> p h c", c=64),
                          ALU.add))
            v_free = p.mark("vector")
            p.wait_mark("gpsimd", v_free)
            lt = lg * 4 + ls
            p.dma("gpsimd", d["vtok_b"][lt * 128:(lt + 1) * 128, :], vasm[:], st)
            vst = p.sems["p1a_st"]
            p.wait("vector", vst)
    identb = p.sb["identb"]
    kabsb = p.sbuf("p1a_kabsb", [128, 1], BF16, tmp=True)
    pstb = p.PSB(0)
    p.run("vector", lambda e: e.tensor_copy(kabsb[:], kabs[:, 0:1]))
    p.wait_mark("tensor", p.mark("vector"))
    p.run("tensor", lambda e: e.transpose(pstb[0:1, 0:128], kabsb[:], identb[:]), mm)
    p.wait("vector", mm)
    kmax = p.sbuf("p1_kmax", [1, 1], F32)
    ktmp = p.sbuf("p1_ktmp", [1, 128], F32)
    p.run("vector", lambda e: e.tensor_copy(ktmp[:], pstb[0:1, 0:128]))
    p.run("vector", lambda e: e.reduce_max(kmax[:], ktmp[:], AX.X))
    p.wait("sync", st)
    p.barrier()
    p.free_tmps()


def emit_ln_rows(p: Prog, rows, src_tile, dst_tile, tag):
    stat = p.sbuf(f"{tag}_stat", [128, 4], F32)
    junk = p.sbuf(f"{tag}_junk", [128, D], F32)
    mu, nmu, var, rstd = (stat[0:rows, i:i + 1] for i in range(4))
    p.run("vector", lambda e: e.reduce_sum(mu, src_tile, AX.X))
    p.run("vector", lambda e: e.tensor_scalar_mul(mu, mu, 1.0 / D))
    p.run("vector", lambda e: e.tensor_scalar(nmu, mu, -1.0, None, ALU.mult))
    p.wait_mark("scalar", p.mark("vector"))
    p.run("scalar", lambda e: e.activation(junk[0:rows, :], src_tile, AF.Square,
                                           bias=nmu, scale=1.0, accum_out=var))
    p.wait_mark("vector", p.mark("scalar"))
    p.run("vector", lambda e: e.tensor_scalar(var, var, 1.0 / D, 1e-5,
                                              ALU.mult, ALU.add))
    p.run("vector", lambda e: e.reciprocal(rstd, var))
    p.wait_mark("scalar", p.mark("vector"))
    p.run("scalar", lambda e: e.sqrt(rstd, rstd))
    p.run("scalar", lambda e: e.activation(junk[0:rows, :], src_tile, AF.Identity,
                                           bias=nmu, scale=1.0))
    p.run("scalar", lambda e: e.mul(dst_tile, junk[0:rows, :], rstd))


def emit_p1b(p: Prog):
    d = p.dram
    wk = p.sb["WkT_sb"]; bk = p.sb["a_bk_sb"]
    identb = p.sb["identb"]
    ld = p.sem("p1b_ld"); mm = p.sem("p1b_mm")
    idxs = p.sbuf("p1b_idx", [U0, 1], I32, tmp=True)
    xs = p.sbuf("p1b_xs", [U0, D], F32, tmp=True)
    xsb = p.sbuf("p1b_xsb", [U0, D], BF16, tmp=True)
    xsf = p.sbuf("p1b_xsf", [128, NCH * U0], BF16, tmp=True)
    ksamp = p.sbuf("ksampf", [128, NCH * U0], BF16)
    pst = p.PSB(0)[:, 0:128]
    psm = p.PS(0)[:, 0:U0]

    p.dma("sync", idxs[:], d["idx_k"], ld)
    p.wait("gpsimd", ld)
    p.run("gpsimd", lambda e: e.indirect_dma_start(
        xs[:], None, d["x_b"], bass.IndirectOffsetOnAxis(idxs[:, 0:1], 0)),
        p.sem("p1b_g"), 16)
    p.wait("vector", p.sem("p1b_g"))
    p.wait("scalar", p.sem("p1b_g"))
    emit_ln_rows(p, U0, xs[:], xsb[:], "p1b")
    sc_done = p.mark("scalar")
    p.wait_mark("tensor", sc_done)
    for c in range(NCH):
        p.run("tensor", lambda e, c=c:
              e.transpose(pst[:, 0:U0], xsb[:, c * 128:(c + 1) * 128], identb[0:U0, 0:U0]), mm)
        p.wait("vector", mm)
        p.run("vector", lambda e, c=c:
              e.tensor_copy(xsf[:, c * U0:(c + 1) * U0], pst[:, 0:U0]))
        p.wait_mark("tensor", p.mark("vector"))
    p.wait_mark("tensor", p.mark("vector"))
    for ic in range(NCH):
        for dc in range(NCH):
            p.run("tensor", lambda e, ic=ic, dc=dc:
                  e.matmul(psm, wk[:, dc * 1024 + ic * 128: dc * 1024 + (ic + 1) * 128],
                           xsf[:, dc * U0:(dc + 1) * U0],
                           start=(dc == 0), stop=(dc == NCH - 1)),
                  mm if dc == NCH - 1 else None)
        p.wait("scalar", mm)
        p.run("scalar", lambda e, ic=ic:
              e.activation(ksamp[:, ic * U0:(ic + 1) * U0], psm, AF.Identity,
                           bias=bk[:, ic:ic + 1], scale=1.0))
        p.wait_mark("tensor", p.mark("scalar"))
    p.barrier()
    p.free_tmps()


def emit_p1c(p: Prog):
    d = p.dram
    ksamp = p.sb["ksampf"]
    identb = p.sb["identb"]; ident32 = p.sb["ident32"]
    ld = p.sem("p1c_ld"); mm = p.sem("p1c_mm"); st = p.sem("p1c_st")
    qh = p.sbuf("p1c_qh", [128, 2 * NCH * 128], BF16, tmp=True)
    Mt8 = p.sbuf("p1c_Mt8", [128, 1024], F32, tmp=True)
    wa = p.sbuf("p1c_wa", [128, 1024], F32, tmp=True)
    wb = p.sbuf("p1c_wb", [128, 1024], F32, tmp=True)
    mcol = p.sbuf("p1c_mcol", [128, 2 * 16], F32, tmp=True)
    mcolb = p.sbuf("p1c_mcolb", [128, 16], BF16, tmp=True)
    tmp16 = p.sbuf("p1c_tmp16", [16, 128], F32, tmp=True)
    pss = [p.PS(0)[:, 0:U0], p.PS(1)[:, 0:U0]]
    pst32f = p.PS(3)[0:16, 0:128]
    pst32 = p.PS(2)[0:64, 0:16]
    red = p.sbuf("p1c_red", [128, 4], F32, tmp=True)

    NLT = L // 128
    qh_busy = [None, None]
    stat_done = [None, None]
    for lt in range(NLT):
        pb = lt % 2
        qtile = qh[:, pb * NCH * 128:(pb + 1) * NCH * 128]
        if qh_busy[pb] is not None:
            p.wait_mark("sync", qh_busy[pb])
        for c in range(NCH):
            p.dma("sync", qtile[:, c * 128:(c + 1) * 128],
                  d["qf_b"][c, :, lt * 128:(lt + 1) * 128], ld)
        p.wait("tensor", ld)
        for h in range(NH0):
            bank = h % 2
            if stat_done[bank] is not None:
                p.wait_mark("tensor", stat_done[bank])
            qsl = qtile.rearrange("p (c n) -> p c n", n=128)
            p.run("tensor", lambda e, h=h, bank=bank, qsl=qsl:
                  e.matmul(pss[bank],
                           qsl[(h % 2) * 64:(h % 2) * 64 + 64, h // 2, :],
                           ksamp.rearrange("p (c u) -> p c u", u=U0)[(h % 2) * 64:(h % 2) * 64 + 64, h // 2, :],
                           start=True, stop=True), mm)
            if h == NH0 - 1:
                qh_busy[pb] = p.mark("tensor")
            p.wait("vector", mm)
            p.run("vector", lambda e, bank=bank: e.reduce_max(red[:, 0:1], pss[bank], AX.X))
            p.run("vector", lambda e, bank=bank: e.reduce_sum(red[:, 1:2], pss[bank], AX.X))
            stat_done[bank] = p.mark("vector")
            p.run("vector", lambda e, h=h, pb=pb:
                  e.tensor_scalar(mcol[:, pb * 16 + h:pb * 16 + h + 1], red[:, 1:2],
                                  -1.0 / U0, None, ALU.mult))
            p.run("vector", lambda e, h=h, pb=pb:
                  e.tensor_tensor(mcol[:, pb * 16 + h:pb * 16 + h + 1],
                                  mcol[:, pb * 16 + h:pb * 16 + h + 1], red[:, 0:1], ALU.add))
        p.fence("vector")
        p.wait_mark("tensor", p.mark("vector"))
        p.run("tensor", lambda e, pb=pb: e.transpose(
            pst32f, mcol[:, pb * 16:(pb + 1) * 16], ident32[:]), mm)
        p.wait("vector", mm)
        p.wait("vector", st)
        p.run("vector", lambda e: e.tensor_copy(tmp16[:], pst32f))
        p.wait_mark("gpsimd", p.mark("vector"))
        seg, off = lt // 8, (lt % 8) * 128
        p.dma("gpsimd", Mt8[seg * 16:(seg + 1) * 16, off:off + 128], tmp16[:], st)

    iota_seg = p.sbuf("p1c_iotaseg", [128, 1024], F32, tmp=True)
    p.dma("sync", iota_seg[:], d["iota_seg"], ld)
    mx8 = p.sbuf("p1c_mx8", [128, 8], F32, tmp=True)
    segv = p.sbuf("p1c_segv", [128, 56], F32, tmp=True)
    segi = p.sbuf("p1c_segi", [128, 56], F32, tmp=True)
    msk = p.sbuf("p1c_msk", [128, 1024], U8, tmp=True)
    thr8 = p.sbuf("p1c_thr8", [128, 1], F32, tmp=True)
    p.wait("vector", st)
    p.wait("vector", ld)

    def rounds(src0, pp0, pp1, out, nr=7, imm=NEG):
        cur = src0
        nxt = pp0
        for r in range(nr):
            p.run("vector", lambda e, s=cur: e.max(out=mx8[:, :], in_=s))
            p.run("vector", lambda e, r=r, out=out:
                  e.tensor_copy(out[:, 8 * r:8 * r + 8], mx8[:, :]))
            if r < nr - 1:
                p.run("vector", lambda e, s=cur, q=nxt, imm=imm:
                      e.match_replace(out=q, in_to_replace=mx8[:, :], in_values=s,
                                      imm_value=imm))
                cur, nxt = nxt, (pp1 if nxt is not pp1 else pp0)

    rounds(Mt8[:], wa[:], wb[:], segv)
    p.run("vector", lambda e: e.tensor_copy(thr8[:], segv[:, 55:56]))
    p.run("vector", lambda e: e.tensor_scalar(msk[:], Mt8[:], thr8[:, 0:1], None, ALU.is_ge))
    p.run("vector", lambda e: e.tensor_scalar_mul(wa[:], iota_seg[:], -1.0))
    p.run("vector", lambda e: e.memset(wb[:], -1e9))
    eb = p.sbuf("p1c_eb", [128, 1024], F32, tmp=True)
    p.run("vector", lambda e: e.select(eb[:], msk[:], wa[:], wb[:]))
    rounds(eb[:], wa[:], wb[:], segi, imm=-2e9)
    p.run("vector", lambda e: e.tensor_scalar_mul(segi[:], segi[:], -1.0))

    candv = p.sbuf("p1c_candv", [16, 8 * 56], F32, tmp=True)
    candi = p.sbuf("p1c_candi", [16, 8 * 56], F32, tmp=True)
    p.wait_mark("gpsimd", p.mark("vector"))
    for seg in range(8):
        p.dma("gpsimd", candv[0:16, seg * 56:(seg + 1) * 56],
              segv[seg * 16:(seg + 1) * 16, :], st)
        p.dma("gpsimd", candi[0:16, seg * 56:(seg + 1) * 56],
              segi[seg * 16:(seg + 1) * 16, :], st)
    p.wait("vector", st)

    cw1 = p.sbuf("p1c_cw1", [16, 448], F32, tmp=True)
    cw2 = p.sbuf("p1c_cw2", [16, 448], F32, tmp=True)
    thr = p.sbuf("p1c_thr", [16, 1], F32)
    cm8 = p.sbuf("p1c_cm8", [16, 8], F32, tmp=True)
    cur, nxt = candv, cw1
    for r in range(7):
        p.run("vector", lambda e, s=cur: e.max(out=cm8[:, :], in_=s[0:16, :]))
        if r == 6:
            p.run("vector", lambda e: e.tensor_copy(thr[:], cm8[:, 1:2]))
        if r < 6:
            p.run("vector", lambda e, s=cur, q=nxt:
                  e.match_replace(out=q[0:16, :], in_to_replace=cm8[:, :],
                                  in_values=s[0:16, :], imm_value=NEG))
            cur, nxt = nxt, (cw2 if nxt is not cw2 else cw1)
    cmsk = p.sbuf("p1c_cmsk", [16, 448], U8, tmp=True)
    idxf = p.sbuf("p1c_idxf", [16, 56], F32, tmp=True)
    p.run("vector", lambda e: e.tensor_scalar(cmsk[0:16, :], candv[0:16, :],
                                              thr[0:16, 0:1], None, ALU.is_ge))
    p.run("vector", lambda e: e.tensor_scalar_mul(cw1[0:16, :], candi[0:16, :], -1.0))
    p.run("vector", lambda e: e.memset(cw2[0:16, :], -1e9))
    eb2 = p.sbuf("p1c_eb2", [16, 448], F32, tmp=True)
    p.run("vector", lambda e: e.select(eb2[0:16, :], cmsk[0:16, :], cw1[0:16, :], cw2[0:16, :]))
    cur, nxt = eb2, cw1
    for r in range(7):
        p.run("vector", lambda e, s=cur: e.max(out=cm8[:, :], in_=s[0:16, :]))
        p.run("vector", lambda e, r=r: e.tensor_copy(idxf[:, 8 * r:8 * r + 8], cm8[:, :]))
        if r < 6:
            p.run("vector", lambda e, s=cur, q=nxt:
                  e.match_replace(out=q[0:16, :], in_to_replace=cm8[:, :],
                                  in_values=s[0:16, :], imm_value=-2e9))
            cur, nxt = nxt, (cw2 if nxt is not cw2 else cw1)
    p.run("vector", lambda e: e.tensor_scalar_mul(idxf[:], idxf[:], -1.0))
    p.wait_mark("tensor", p.mark("vector"))
    p.run("tensor", lambda e: e.transpose(pst32[0:56, :], idxf[:], ident32[0:16, 0:16]), mm)
    idxg = p.sbuf("idxg", [56, 16], F32, tmp=True)
    idx_i32 = p.sbuf("idx_i32", [56, 16], I32)
    idxsc_f = p.sbuf("idxsc_f", [56, 16], F32, tmp=True)
    idx_sc = p.sbuf("idx_sc", [56, 16], I32)
    hof = p.sbuf("p1c_hof", [128, 1], F32, tmp=True)
    mneg = p.sbuf("p1c_mneg", [56, 16], U8, tmp=True)
    bigs = p.sbuf("p1c_bigs", [56, 16], F32, tmp=True)
    p.dma("sync", hof[:], d["half_off"], ld)
    p.wait("vector", mm)
    p.wait("vector", ld)
    p.run("vector", lambda e: e.tensor_copy(idxg[:], pst32[0:56, :]))
    p.run("vector", lambda e: e.tensor_copy(idx_i32[:], idxg[:]))
    p.run("vector", lambda e: e.tensor_scalar(idxsc_f[:], idxg[:], hof[0:56, 0:1], None, ALU.subtract))
    p.run("vector", lambda e: e.tensor_scalar(mneg[:], idxsc_f[:], 0.0, None, ALU.is_lt))
    p.run("vector", lambda e: e.memset(bigs[:], float(T)))
    p.run("vector", lambda e: e.select(idxsc_f[:], mneg[:], bigs[:], idxsc_f[:]))
    p.run("vector", lambda e: e.tensor_scalar_min(idxsc_f[:], idxsc_f[:], float(T)))
    p.run("vector", lambda e: e.tensor_copy(idx_sc[:], idxsc_f[:]))
    if DEBUG_P1C:
        dbs = p.sem("dbgc")
        p.wait_mark("gpsimd", p.mark("vector"))
        p.dma("gpsimd", d["carry_gath"][0:1, 0:16].rearrange("a b -> (a b)").rearrange("(a b) -> b a", a=16), thr[:], dbs, allow_slow_non_contiguous=True)
        p.dma("gpsimd", d["carry_gath"][0, 100:548].rearrange("(a b) -> a b", a=1), candv[0:1, :], dbs)
        p.dma("gpsimd", d["carry_gath"][0, 600:1048].rearrange("(a b) -> a b", a=1), candi[0:1, :], dbs)
        p.dma("gpsimd", d["carry_gath"][1, 0:56].rearrange("(a b) -> a b", a=1), idxf[0:1, :], dbs)
        p.wait("gpsimd", dbs)
    p.barrier()
    p.free_tmps()


P1DE_NHEADS = NH0
DEBUG_P1C = False
DEBUG_P5 = False
DEBUG_P1DE = False


def emit_p1de(p: Prog):
    d = p.dram
    wq = p.sb["WqT_sb"]; wo = p.sb["WoT_sb"]
    bq = p.sb["a_bq_sb"]; bo_row = p.sb["bo_row"]
    identb = p.sb["identb"]; ident32 = p.sb["ident32"]
    idx_i32 = p.sb["idx_i32"]; idx_sc = p.sb["idx_sc"]
    kmax = p.sb["p1_kmax"]
    onesr32 = p.sb["ones_1x128"]
    ld = p.sem("p1d_ld"); mm = p.sem("p1d_mm"); st = p.sem("p1d_st")
    gs = p.sem("p1d_gs")

    onescol = p.sbuf("p1d_onescol", [128, 1], BF16)
    vt = p.sbuf("p1d_vt", [128, 2 * 1040], BF16)
    psvm = [p.PS(0)[0:1, :], p.PS(1)[0:1, :], p.PS(2)[0:1, 0:16]]
    vmean = p.sbuf("p1d_vmean", [1, 1040], F32)
    vacc = p.sbuf("p1d_vacc", [1, 1040], F32)
    p.run("vector", lambda e: e.memset(onescol[:], 1.0))
    p.run("vector", lambda e: e.memset(vacc[:], 0.0))
    vbusy = [None, None]
    psfree = [None, None, None]
    NLT = L // 128
    for lt in range(NLT):
        pb = lt % 2
        if vbusy[pb] is not None:
            p.wait_mark("sync", vbusy[pb])
        p.dma("sync", vt[:, pb * 1040:(pb + 1) * 1040],
              d["vtok_b"][lt * 128:(lt + 1) * 128, :], ld)
        p.wait("tensor", ld)
        if lt == 0:
            p.wait_mark("tensor", p.mark("vector"))
        for seg, (o0, n) in enumerate(((0, 512), (512, 512), (1024, 16))):
            if psfree[seg] is not None:
                p.wait_mark("tensor", psfree[seg])
            p.run("tensor", lambda e, pb=pb, seg=seg, o0=o0, n=n:
                  e.matmul(psvm[seg][0:1, 0:n], onescol[:],
                           vt[:, pb * 1040 + o0: pb * 1040 + o0 + n],
                           start=True, stop=True), mm)
        vbusy[pb] = p.mark("tensor")
        p.wait("vector", mm)
        for seg, (o0, n) in enumerate(((0, 512), (512, 512), (1024, 16))):
            p.run("vector", lambda e, seg=seg, o0=o0, n=n:
                  e.tensor_tensor(vacc[0:1, o0:o0 + n], vacc[0:1, o0:o0 + n],
                                  psvm[seg][0:1, 0:n], ALU.add))
            psfree[seg] = p.mark("vector")
    p.run("vector", lambda e: e.tensor_scalar_mul(vmean[:], vacc[:], 1.0 / L))
    vm_done = p.mark("vector")

    vmf = p.sbuf("p1d_vmf", [128, NCH], F32)
    vmb = p.sbuf("p1d_vmb", [1, 1040], BF16)
    pstf = p.PSB(0)[0:64, 0:1]
    p.run("vector", lambda e: e.tensor_copy(vmb[:], vmean[:]))
    p.wait_mark("tensor", p.mark("vector"))
    for h in range(NH0):
        p.run("tensor", lambda e, h=h:
              e.transpose(pstf, vmb[0:1, h * 65:h * 65 + 64], identb[0:1, 0:1]), mm)
        p.wait("vector", mm)
        p.run("vector", lambda e, h=h:
              e.tensor_copy(vmf[(h % 2) * 64:(h % 2) * 64 + 64, h // 2:h // 2 + 1],
                            pstf))
        p.wait_mark("tensor", p.mark("vector"))
    vmfb = p.sbuf("p1d_vmfb", [128, NCH], BF16)
    p.run("vector", lambda e: e.tensor_copy(vmfb[:], vmf[:]))
    if DEBUG_P1DE:
        p.wait_mark("scalar", p.mark("vector"))
        p.dma("scalar", p.dram["carry_gath"][0:1, 0:1040], vmean[:], p.sem("dbg"))
        p.dma("scalar", p.dram["carry_gath"][1, 0:1024].rearrange("(p c) -> p c", p=128),
              vmf[:], p.sem("dbg"))
        p.wait("scalar", p.sem("dbg"))
    p.wait_mark("tensor", p.mark("vector"))

    psr = [p.PS(3)[0:1, :], p.PS(4)[0:1, :]]
    for half in range(2):
        for dc in range(NCH):
            p.run("tensor", lambda e, half=half, dc=dc:
                  e.matmul(psr[half], vmfb[:, dc:dc + 1],
                           wo[:, dc * 1024 + half * 512: dc * 1024 + (half + 1) * 512],
                           start=(dc == 0), stop=(dc == NCH - 1)),
                  mm if dc == NCH - 1 else None)
    r0 = p.sbuf("p1d_r0row", [1, D], F32)
    p.wait("vector", mm)
    for half in range(2):
        p.run("vector", lambda e, half=half:
              e.tensor_copy(r0[0:1, half * 512:(half + 1) * 512], psr[half]))
    p.run("vector", lambda e: e.tensor_add(r0[:], r0[:], bo_row[:]))
    r0_done = p.mark("vector")
    p.wait_mark("scalar", r0_done)
    p.dma("scalar", d["carry_out"][0:1, 0:D], r0[:], st)
    p.wait("sync", st)
    filltile = p.sbuf("p1d_fill", [128, D], F32)
    p.dma("sync", filltile[:],
          d["carry_out"][0:1, 0:D].to_broadcast([128, D]), ld)
    p.wait("sync", ld)
    for i in range(T // 128):
        p.dma("sync", d["attnout"][i * 128:(i + 1) * 128, :], filltile[:], st)

    xs = p.sbuf("p1d_xs", [U0, D], F32)
    xsb = p.sbuf("p1d_xsb", [U0, D], BF16)
    xsf = p.sbuf("p1d_xsf", [128, NCH * U0], BF16)
    pst = p.PSB(0)[:, 0:U0]
    psq = p.PS(0)[:, 0:U0]
    qtop = p.sbuf("p1d_qtop", [65, U0], BF16)
    qtf = p.sbuf("p1d_qtf", [128, U0], F32)
    qsqb = p.sbuf("p1d_qsqb", [64, U0], BF16)
    psqn = p.PS(2)[0:1, 0:U0]
    qn = p.sbuf("p1d_qn", [1, U0], F32)
    augr = p.sbuf("p1d_augr", [1, U0], BF16)
    kafull = p.sbuf("p1d_kafull", [65, L], BF16)
    pstf2 = p.PSB(1)[0:1, 0:U0]
    vtile = p.sbuf("p1d_vtile", [128, 2 * 65], BF16)
    ptile = p.sbuf("p1d_ptile", [128, 2 * U0], BF16)
    pss2 = [p.PS(3)[:, 0:U0], p.PS(4)[:, 0:U0]]
    psctx = p.PS(5)[0:U0, 0:65]
    ctxacc = p.sbuf("p1d_ctxacc", [U0, 65], F32)
    ctx = p.sbuf("p1d_ctxs", [U0, 64], F32)
    ctxb = p.sbuf("p1d_ctxb", [U0, 64], BF16)
    ctxTb = p.sbuf("p1d_ctxTb", [128, U0], BF16)
    pspj = [p.PS(0)[0:U0, :], p.PS(1)[0:U0, :]]
    proj = p.sbuf("p1d_proj", [U0, D], F32)

    xs_free = None
    ps0_free = None
    kaf_free = None
    for h in range(P1DE_NHEADS):
        if xs_free is not None:
            p.wait_mark("gpsimd", xs_free)
        p.run("gpsimd", lambda e, h=h: e.indirect_dma_start(
            xs[0:U0, :], None, d["x_b"],
            bass.IndirectOffsetOnAxis(idx_i32[0:U0, h:h + 1], 0)), gs, 16)
        p.wait("vector", gs)
        p.wait("scalar", gs)
        emit_ln_rows(p, U0, xs[0:U0, :], xsb[0:U0, :], "p1d")
        xs_free = p.mark("scalar")
        p.wait_mark("tensor", p.mark("scalar"))
        for c in range(NCH):
            p.run("tensor", lambda e, c=c:
                  e.transpose(pst, xsb[:, c * 128:(c + 1) * 128],
                              identb[0:U0, 0:U0]), mm)
            p.wait("vector", mm)
            p.run("vector", lambda e, c=c:
                  e.tensor_copy(xsf[:, c * U0:(c + 1) * U0], pst))
            p.wait_mark("tensor", p.mark("vector"))
        if ps0_free is not None:
            p.wait_mark("tensor", ps0_free)
        for dc in range(NCH):
            p.run("tensor", lambda e, h=h, dc=dc:
                  e.matmul(psq,
                           wq[:, dc * 1024 + (h // 2) * 128: dc * 1024 + (h // 2) * 128 + 128],
                           xsf[:, dc * U0:(dc + 1) * U0],
                           start=(dc == 0), stop=(dc == NCH - 1)),
                  mm if dc == NCH - 1 else None)
        p.wait("vector", mm)
        p.run("vector", lambda e, h=h:
              e.tensor_scalar(qtf[:], psq, bq[:, h // 2:h // 2 + 1],
                              0.125, ALU.add, ALU.mult))
        ps0_free = p.mark("vector")
        p.run("vector", lambda e, h=h:
              e.tensor_copy(qtop[0:64, :], qtf[(h % 2) * 64:(h % 2) * 64 + 64, :]))
        qtop_ready = p.mark("vector")
        if kaf_free is not None:
            p.wait_mark("sync", kaf_free)
        for lg in range(L // 512):
            p.dma("sync", kafull[:, lg * 512:(lg + 1) * 512],
                  d["kfa_b"][h, :, lg * 512:(lg + 1) * 512], ld)
        p.run("vector", lambda e: e.memset(qtop[64:65, :], 0.0))
        p.wait_mark("tensor", p.mark("vector"))
        p.wait("tensor", ld)
        psA = p.PS(3)
        smax_c = p.sbuf("p1d_smaxc", [U0, 2], F32)
        smax_r = p.sbuf("p1d_smaxr", [1, U0], F32)
        smax_rb = p.sbuf("p1d_smaxrb", [1, U0], BF16)
        first = True
        for lg in range(L // 512):
            bank = lg % 2
            if lg >= 2:
                p.wait_mark("tensor", amax_done[bank])
            p.run("tensor", lambda e, lg=lg, bank=bank:
                  e.matmul(p.PS(3 + bank)[0:U0, :], qtop[:],
                           kafull[:, lg * 512:(lg + 1) * 512],
                           start=True, stop=True), mm)
            p.wait("vector", mm)
            p.run("vector", lambda e, bank=bank:
                  e.reduce_max(smax_c[0:U0, 1:2], p.PS(3 + bank)[0:U0, :], AX.X))
            if lg == 0:
                amax_done = [None, None]
                p.run("vector", lambda e: e.tensor_copy(smax_c[0:U0, 0:1], smax_c[0:U0, 1:2]))
            else:
                p.run("vector", lambda e: e.tensor_tensor(
                    smax_c[0:U0, 0:1], smax_c[0:U0, 0:1], smax_c[0:U0, 1:2], ALU.max))
            amax_done[bank] = p.mark("vector")
        smaxb = p.sbuf("p1d_smaxb", [U0, 1], BF16)
        p.run("vector", lambda e: e.tensor_copy(smaxb[0:U0, :], smax_c[0:U0, 0:1]))
        p.wait_mark("tensor", p.mark("vector"))
        p.run("tensor", lambda e: e.transpose(pstf2, smaxb[0:U0, :], identb[0:U0, 0:U0]), mm)
        p.wait("vector", mm)
        p.run("vector", lambda e: e.tensor_scalar(smax_rb[0:1, :], pstf2, -1.0, None, ALU.mult))
        p.wait_mark("gpsimd", p.mark("vector"))
        p.dma("gpsimd", qtop[64:65, :], smax_rb[0:1, :], gs)
        p.wait("tensor", gs)

        vt_busy = [None, None]
        pt_free = [None, None]
        ctx_free = None
        p.run("vector", lambda e: e.memset(ctxacc[0:U0, :], 0.0))
        for lt in range(NLT):
            pb = lt % 2
            if vt_busy[pb] is not None:
                p.wait_mark("sync", vt_busy[pb])
            p.dma("sync", vtile[:, pb * 65:(pb + 1) * 65],
                  d["vtok_b"][lt * 128:(lt + 1) * 128, h * 65:(h + 1) * 65], ld)
            p.wait("tensor", ld)
            if pt_free[pb] is not None:
                p.wait_mark("tensor", pt_free[pb])
            p.run("tensor", lambda e, pb=pb, lt=lt:
                  e.matmul(pss2[pb], kafull[:, lt * 128:(lt + 1) * 128],
                           qtop[:], start=True, stop=True), mm)
            p.wait("scalar", mm)
            p.run("scalar", lambda e, pb=pb:
                  e.activation(ptile[:, pb * U0:(pb + 1) * U0], pss2[pb], AF.Exp))
            pt_free[pb] = p.mark("scalar")
            p.wait_mark("tensor", pt_free[pb])
            if ctx_free is not None:
                p.wait_mark("tensor", ctx_free)
            p.run("tensor", lambda e, pb=pb:
                  e.matmul(psctx, ptile[:, pb * U0:(pb + 1) * U0],
                           vtile[:, pb * 65:(pb + 1) * 65],
                           start=True, stop=True), mm)
            vt_busy[pb] = p.mark("tensor")
            if lt == NLT - 1:
                kaf_free = p.mark("tensor")
            p.wait("vector", mm)
            p.run("vector", lambda e:
                  e.tensor_tensor(ctxacc[0:U0, :], ctxacc[0:U0, :],
                                  psctx[0:U0, :], ALU.add))
            ctx_free = p.mark("vector")
        rden = p.sbuf("p1d_rden", [U0, 1], F32)
        p.run("vector", lambda e: e.reciprocal(rden[0:U0, :], ctxacc[0:U0, 64:65]))
        p.run("vector", lambda e:
              e.tensor_scalar(ctx[0:U0, :], ctxacc[0:U0, 0:64], rden[0:U0, 0:1],
                              None, ALU.mult))
        p.run("vector", lambda e: e.tensor_copy(ctxb[0:U0, :], ctx[0:U0, :]))
        ctx_done = p.mark("vector")
        p.wait_mark("tensor", ctx_done)
        p.run("tensor", lambda e:
              e.transpose(pst[0:64, 0:U0], ctxb[0:U0, :], identb[0:U0, 0:U0]), mm)
        p.wait("vector", mm)
        hb = (h % 2) * 64
        p.run("vector", lambda e, h=h, hb=hb:
              e.tensor_scalar(ctxTb[hb:hb + 64, :], pst[0:64, 0:U0],
                              vmf[hb:hb + 64, h // 2:h // 2 + 1],
                              None, ALU.subtract))
        p.wait_mark("tensor", p.mark("vector"))
        for half in range(2):
            p.run("tensor", lambda e, h=h, half=half, hb=hb:
                  e.matmul(pspj[half], ctxTb[hb:hb + 64, :],
                           wo[hb:hb + 64,
                              (h // 2) * 1024 + half * 512:(h // 2) * 1024 + (half + 1) * 512],
                           start=True, stop=True),
                  mm if half == 1 else None)
        p.wait("vector", mm)
        for half in range(2):
            p.run("vector", lambda e, half=half:
                  e.tensor_copy(proj[0:U0, half * 512:(half + 1) * 512], pspj[half]))
        proj_done = p.mark("vector")
        p.wait_mark("gpsimd", proj_done)
        p.wait("gpsimd", st)
        p.run("gpsimd", lambda e, h=h: e.indirect_dma_start(
            d["attnout"], bass.IndirectOffsetOnAxis(idx_sc[0:U0, h:h + 1], 0),
            proj[0:U0, :], None, compute_op=ALU.add), st, 16)
        p.wait("gpsimd", st)
        if DEBUG_P1DE and h == 0:
            p.dma("gpsimd", d["carry_gath"][0:1, 0:1024], proj[0:1, :], st)
            p.dma("gpsimd", d["carry_gath"][0:1, 1024:1025], kmax[0:1, :], st)
            p.dma("gpsimd", d["carry_gath"][0:1, 1100:1150], qn[0:1, :], st)
            dbga = p.sbuf("dbg_aug", [1, U0], F32)
            p.run("vector", lambda e: e.tensor_copy(dbga[:], qtop[64:65, :]))
            p.wait_mark("gpsimd", p.mark("vector"))
            p.dma("gpsimd", d["carry_gath"][0:1, 1200:1250], dbga[0:1, :], st)
            dbgc = p.sbuf("dbg_ctx", [1, 65], F32)
            p.run("vector", lambda e: e.tensor_copy(dbgc[:], ctxacc[0:1, :]))
            p.wait_mark("gpsimd", p.mark("vector"))
            p.dma("gpsimd", d["carry_gath"][0:1, 1300:1365], dbgc[0:1, :], st)
            isf = p.sbuf("dbg_isf", [56, 16], F32)
            p.run("vector", lambda e: e.tensor_copy(isf[:], idx_sc[:]))
            p.wait_mark("gpsimd", p.mark("vector"))
            p.dma("gpsimd", d["carry_gath"][1, 0:56 * 16].rearrange("(a b) -> a b", a=56),
                  isf[:], st)
            p.wait("gpsimd", st)
        ps0_free = p.mark("gpsimd")
    p.barrier()



P2_RNG = [
    {"A": (2, 45), "B": (6, 45), "C1": (6, 45), "C2": (7, 45)},
    {"A": (8, 45), "B": (12, 45), "C1": (12, 45), "C2": (13, 45)},
]


def _groups(rng):
    c0, c1 = rng[0] * 128, rng[1] * 128
    out = []
    c = c0
    while c < c1:
        n = min(512, c1 - c)
        out.append((c, n))
        c += n
    return out


def emit_p2_rotary(p, li, side, ps, dst, rc_t, rs_t, sc_tile, tag):
    sq = p.sbuf(f"p2r_sq{tag}", [128, 512], F32, tmp=True)
    n2 = p.sbuf(f"p2r_n2{tag}", [128, 8], F32, tmp=True)
    t2 = p.sbuf(f"p2r_t2{tag}", [128, 512], F32, tmp=True)
    rot = p.sbuf(f"p2r_rot{tag}", [128, 512], F32, tmp=True)
    v3 = lambda t: t.rearrange("p (h i) -> p h i", i=FDK)
    p.wait_mark("scalar", p.mark("tensor"))
    p.run("scalar", lambda e: e.activation(sq[:], ps, AF.Square))
    p.wait_mark("vector", p.mark("scalar"))
    p.run("vector", lambda e: e.tensor_reduce(n2[:], v3(sq), AX.X, ALU.add))
    p.wait_mark("scalar", p.mark("vector"))
    p.run("scalar", lambda e: e.sqrt(n2[:], n2[:]))
    p.wait_mark("vector", p.mark("scalar"))
    p.run("vector", lambda e: e.tensor_scalar_max(n2[:], n2[:], 1e-12))
    p.run("vector", lambda e: e.reciprocal(n2[:], n2[:]))
    p.run("vector", lambda e: e.tensor_tensor(
        v3(t2), v3(ps.rearrange("p n -> p n") if False else ps),
        n2.rearrange("p (h o) -> p h o", o=1).to_broadcast([128, 8, FDK]), ALU.mult))
    p.run("vector", lambda e: e.tensor_tensor(
        v3(t2), v3(t2),
        sc_tile.rearrange("p (o i) -> p o i", o=1).to_broadcast([128, 8, FDK]), ALU.mult))
    p.run("vector", lambda e: e.tensor_copy(
        rot.rearrange("p (h s i) -> p h s i", s=2, i=32)[:, :, 0, :],
        t2.rearrange("p (h s i) -> p h s i", s=2, i=32)[:, :, 1, :]))
    p.run("vector", lambda e: e.tensor_copy(
        rot.rearrange("p (h s i) -> p h s i", s=2, i=32)[:, :, 1, :],
        t2.rearrange("p (h s i) -> p h s i", s=2, i=32)[:, :, 0, :]))
    p.run("vector", lambda e: e.tensor_tensor(
        v3(t2), v3(t2), rc_t.rearrange("p (o i) -> p o i", o=1).to_broadcast([128, 8, FDK]),
        ALU.mult))
    p.run("vector", lambda e: e.tensor_tensor(
        v3(rot), v3(rot), rs_t.rearrange("p (o i) -> p o i", o=1).to_broadcast([128, 8, FDK]),
        ALU.mult))
    p.run("vector", lambda e: e.tensor_add(dst, t2[:], rot[:]))


def emit_p2_stageA(p, li):
    d = p.dram
    t0, t1 = P2_RNG[li]["A"]
    ld = p.sem(f"p2a{li}_ld"); mm = p.sem(f"p2a{li}_mm"); st = p.sem(f"p2a{li}_st")
    wq = p.sbuf(f"p2w{li}_wqkv", [128, NCH * 3 * INNER], BF16, tmp=True)
    for c in range(NCH):
        p.dma("gpsimd", wq[:, c * 1536:(c + 1) * 1536],
              d[f"f{li}_wqkv"][c * 128:(c + 1) * 128, :], p.sem(f"p2a{li}_wld"))
    onesr = p.sb["ones_1x128"]
    qsr = p.sbuf(f"p2a{li}_qsr", [1, 2 * FDK], F32, tmp=True)
    p.dma("sync", qsr[0:1, 0:FDK], d[f"f{li}_qs"], ld)
    p.dma("sync", qsr[0:1, FDK:], d[f"f{li}_ks"], ld)
    sc_q = p.sbuf(f"p2a{li}_scq", [128, FDK], F32, tmp=True)
    sc_k = p.sbuf(f"p2a{li}_sck", [128, FDK], F32, tmp=True)
    p.wait("tensor", ld)
    p.run("tensor", lambda e: e.matmul(p.PS(0)[:, 0:2 * FDK], onesr[:], qsr[:],
                                       start=True, stop=True), mm)
    p.wait("vector", mm)
    p.run("vector", lambda e: e.tensor_scalar_mul(sc_q[:], p.PS(0)[:, 0:FDK], 8.0))
    p.run("vector", lambda e: e.tensor_copy(sc_k[:], p.PS(0)[:, FDK:2 * FDK]))

    xw = p.sbuf(f"p2a{li}_xw", [128, NCH * 136], BF16, tmp=True)
    rott = p.sbuf(f"p2a{li}_rot", [128, 4 * FDK], BF16, tmp=True)
    qhat = p.sbuf(f"p2a{li}_qh", [128, 512], BF16, tmp=True)
    khat = p.sbuf(f"p2a{li}_kh", [128, 512], BF16, tmp=True)
    vev = p.sbuf(f"p2a{li}_vev", [128, 512], BF16, tmp=True)
    tev = p.sbuf(f"p2a{li}_tev", [128, 128], BF16, tmp=True)
    psq, psk, psv = p.PS(1), p.PS(2), p.PS(3)
    pst = p.PSB(0)[:, 0:128]
    xsrc = "xnf_l" if li == 0 else "xout_b"
    xidx = (lambda c: (c,)) if li == 0 else (lambda c: (0, c))
    p.wait("tensor", p.sem(f"p2a{li}_wld"))
    rot_mark = vev_mark = None
    for lt in range(t0, t1):
        c0 = lt * 128
        p.wait_mark("sync", p.mark("tensor"))
        if rot_mark is not None:
            p.wait_mark("tensor", rot_mark)
            p.wait_mark("tensor", vev_mark)
        for c in range(NCH):
            p.dma("sync", xw[:, c * 136:c * 136 + 136],
                  d[xsrc][xidx(c) + (slice(0, 128), slice(c0 - 8, c0 + 128))], ld)
        p.dma("gpsimd", rott[:, 0:FDK], d["rot_qc"][c0:c0 + 128, :], ld)
        p.dma("gpsimd", rott[:, FDK:2 * FDK], d["rot_qs"][c0:c0 + 128, :], ld)
        p.dma("gpsimd", rott[:, 2 * FDK:3 * FDK], d["rot_kc"][c0:c0 + 128, :], ld)
        p.dma("gpsimd", rott[:, 3 * FDK:], d["rot_ks"][c0:c0 + 128, :], ld)
        p.wait("tensor", ld)
        for sel, ps in ((0, psq), (1, psk), (2, psv)):
            for c in range(NCH):
                off = 8 if c < 4 else 7
                p.run("tensor", lambda e, c=c, sel=sel, ps=ps, off=off:
                      e.matmul(ps[:], xw[:, c * 136 + off:c * 136 + off + 128],
                               wq[:, c * 1536 + sel * 512:(c) * 1536 + (sel + 1) * 512],
                               start=(c == 0), stop=(c == NCH - 1)),
                      mm if c == NCH - 1 else None)
        p.wait("scalar", mm)
        p.wait("scalar", st)
        p.run("scalar", lambda e: e.copy(vev[:], psv[:]))
        vev_mark = p.mark("scalar")
        p.wait_mark("gpsimd", vev_mark)
        p.dma("gpsimd", d["vtok_l"][c0:c0 + 128, :], vev[:], st)
        emit_p2_rotary(p, li, "q", psq[:], qhat[:], rott[:, 0:FDK],
                       rott[:, FDK:2 * FDK], sc_q, "q")
        emit_p2_rotary(p, li, "k", psk[:], khat[:], rott[:, 2 * FDK:3 * FDK],
                       rott[:, 3 * FDK:], sc_k, "k")
        rot_mark = p.mark("vector")
        p.wait_mark("tensor", p.mark("vector"))
        for src_t, dstn in ((qhat, "qaf_l"), (khat, "kaf_l")):
            for c in range(4):
                p.run("tensor", lambda e, src_t=src_t, c=c:
                      e.transpose(pst, src_t[:, c * 128:(c + 1) * 128], p.sb["identb"][:]),
                      mm)
                p.wait("vector", mm)
                p.wait("vector", st)
                p.run("vector", lambda e: e.tensor_copy(tev[:], pst))
                p.wait_mark("gpsimd", p.mark("vector"))
                for hh in range(2):
                    h = 2 * c + hh
                    p.dma("gpsimd", d[dstn][h, 0:64, c0:c0 + 128],
                          tev[hh * 64:(hh + 1) * 64, :], st)
                p.wait_mark("tensor", p.mark("vector"))
    vrow = p.sbuf(f"p2a{li}_vrow", [1, LT], BF16, tmp=True)
    orow = p.sbuf(f"p2a{li}_orow", [1, LT], BF16, tmp=True)
    p.dma("gpsimd", vrow[:], d["validity"], ld)
    p.wait("vector", ld)
    p.run("vector", lambda e: e.memset(orow[:], 1.0))
    p.wait_mark("gpsimd", p.mark("vector"))
    for h in range(FH):
        p.dma("gpsimd", d["qaf_l"][h, 64:65, :], orow[:], st)
        p.dma("gpsimd", d["kaf_l"][h, 64:65, :], vrow[:], st)
    p.wait("sync", st)
    p.wait("gpsimd", st)
    p.barrier()
    p.free_tmps()


def emit_p2_stageB(p, li):
    d = p.dram
    t0, t1 = P2_RNG[li]["B"]
    ld = p.sem(f"p2b{li}_ld"); mm = p.sem(f"p2b{li}_mm"); st = p.sem(f"p2b{li}_st")
    band = p.sbuf("band_sb", [128, 640], F32)
    if not getattr(p, "_band_loaded", False):
        p.dma("sync", band[:], d["band"], ld)
        p._band_loaded = True
    qa = p.sbuf(f"p2b_qa", [65, LT], BF16, tmp=True)
    ka = p.sbuf(f"p2b_ka", [65, LT], BF16, tmp=True)
    vh = p.sbuf(f"p2b_vh", [128, NT_L * FDK], BF16, tmp=True)
    sp = p.sbuf(f"p2b_sp", [128, 640], F32, tmp=True)
    Pt = p.sbuf(f"p2b_P", [128, 640], BF16, tmp=True)
    PT = p.sbuf(f"p2b_PT", [128, 5 * 128], BF16, tmp=True)
    oev = p.sbuf(f"p2b_oev", [128, 64], BF16, tmp=True)
    ofin = p.sbuf(f"p2b_ofin", [64, 128], BF16, tmp=True)
    red = p.sbuf(f"p2b_red", [128, 3], F32, tmp=True)
    psS1, psS2 = p.PS(0), p.PS(1)[:, 0:128]
    psC = p.PS(2)[:, 0:64]
    pstp = p.PSB(0)[:, 0:128]
    psto = p.PSB(1)[0:64, 0:128]
    for h in range(FH):
        p.wait_mark("sync", p.mark("tensor"))
        p.dma("sync", qa[:], d["qaf_l"][h], ld)
        p.dma("sync", ka[:], d["kaf_l"][h], ld)
        p.dma("sync", vh[:],
              d["vtok_l"].rearrange("(t p) c -> p t c", p=128)[:, :, h * FDK:(h + 1) * FDK],
              ld)
        p.wait("tensor", ld)
        spd = None
        for qt in range(t0, t1):
            q0 = qt * 128
            j0 = q0 - 512
            if spd is not None:
                p.wait_mark("tensor", spd)
            p.run("tensor", lambda e, q0=q0, j0=j0:
                  e.matmul(psS1[:], qa[:, q0:q0 + 128], ka[:, j0:j0 + 512],
                           start=True, stop=True))
            p.run("tensor", lambda e, q0=q0, j0=j0:
                  e.matmul(psS2, qa[:, q0:q0 + 128], ka[:, j0 + 512:j0 + 640],
                           start=True, stop=True), mm)
            p.wait("vector", mm)
            p.run("vector", lambda e: e.tensor_tensor(sp[:, 0:512], psS1[:],
                                                      band[:, 0:512], ALU.add))
            p.run("vector", lambda e: e.tensor_tensor(sp[:, 512:640], psS2,
                                                      band[:, 512:640], ALU.add))
            spd = p.mark("vector")
            p.run("vector", lambda e: e.reduce_max(red[:, 0:1], sp[:], AX.X))
            p.run("vector", lambda e: e.tensor_scalar_mul(red[:, 1:2], red[:, 0:1], -1.0))
            p.wait_mark("scalar", p.mark("vector"))
            p.run("scalar", lambda e: e.activation(Pt[:], sp[:], AF.Exp,
                                                   bias=red[:, 1:2], scale=1.0,
                                                   accum_out=red[:, 2:3]))
            p.wait_mark("vector", p.mark("scalar"))
            p.run("vector", lambda e: e.reciprocal(red[:, 2:3], red[:, 2:3]))
            rden_done = p.mark("vector")
            p.wait_mark("tensor", p.mark("scalar"))
            for c in range(5):
                p.run("tensor", lambda e, c=c:
                      e.transpose(pstp, Pt[:, c * 128:(c + 1) * 128], p.sb["identb"][:]),
                      mm)
                p.wait("scalar", mm)
                p.run("scalar", lambda e, c=c:
                      e.copy(PT[:, c * 128:(c + 1) * 128], pstp))
                p.wait_mark("tensor", p.mark("scalar"))
            for c in range(5):
                jt = qt - 4 + c
                p.run("tensor", lambda e, c=c, jt=jt:
                      e.matmul(psC, PT[:, c * 128:(c + 1) * 128],
                               vh[:, jt * FDK:(jt + 1) * FDK],
                               start=(c == 0), stop=(c == 4)),
                      mm if c == 4 else None)
            p.wait("vector", mm)
            p.run("vector", lambda e: e.tensor_scalar(oev[:], psC, red[:, 2:3],
                                                      None, ALU.mult))
            p.wait_mark("tensor", p.mark("vector"))
            p.run("tensor", lambda e:
                  e.transpose(psto, oev[:], p.sb["identb"][:]), mm)
            p.wait("scalar", mm)
            p.wait("scalar", st)
            p.run("scalar", lambda e: e.copy(ofin[:], psto))
            p.wait_mark("gpsimd", p.mark("scalar"))
            p.dma("gpsimd", d["of_l"][h * 64:(h + 1) * 64, q0:q0 + 128], ofin[:], st)
            p.wait_mark("tensor", p.mark("scalar"))
    p.wait("gpsimd", st)
    p.barrier()
    p.free_tmps()


def emit_p2_stageC1(p, li):
    d = p.dram
    ld = p.sem(f"p2c1{li}_ld"); mm = p.sem(f"p2c1{li}_mm"); st = p.sem(f"p2c1{li}_st")
    wo = p.sbuf(f"p2c1{li}_wo", [128, 4 * 1024], BF16, tmp=True)
    for ic in range(4):
        p.dma("gpsimd", wo[:, ic * 1024:(ic + 1) * 1024],
              d[f"f{li}_wo"][ic * 128:(ic + 1) * 128, :], ld)
    of = p.sbuf(f"p2c1_of", [128, 4 * 512], BF16, tmp=True)
    xin = p.sbuf(f"p2c1_xin", [128, 512], BF16, tmp=True)
    xf = p.sbuf(f"p2c1_xf", [128, 512], F32, tmp=True)
    xb = p.sbuf(f"p2c1_xb", [128, 512], BF16, tmp=True)
    psO = [p.PS(0), p.PS(1)]
    xsrc = "xnf_l" if li == 0 else "xout_b"
    xidx = (lambda c: (c,)) if li == 0 else (lambda c: (0, c))
    p.wait("tensor", ld)
    psfree = [None, None]
    for (g0, gn) in _groups(P2_RNG[li]["C1"]):
        p.wait_mark("sync", p.mark("tensor"))
        for ic in range(4):
            p.dma("sync", of[:, ic * 512:ic * 512 + gn],
                  d["of_l"][ic * 128:(ic + 1) * 128, g0:g0 + gn], ld)
        p.wait("tensor", ld)
        for dc in range(NCH):
            bank = dc % 2
            if psfree[bank] is not None:
                p.wait_mark("tensor", psfree[bank])
            for ic in range(4):
                p.run("tensor", lambda e, ic=ic, dc=dc, bank=bank, gn=gn:
                      e.matmul(psO[bank][:, 0:gn],
                               wo[:, ic * 1024 + dc * 128:ic * 1024 + (dc + 1) * 128],
                               of[:, ic * 512:ic * 512 + gn],
                               start=(ic == 0), stop=(ic == 3)),
                      mm if ic == 3 else None)
            p.wait_mark("gpsimd", p.mark("vector"))
            p.dma("gpsimd", xin[:, 0:gn], d[xsrc][xidx(dc) + (slice(0, 128), slice(g0, g0 + gn))], ld)
            p.wait("vector", ld)
            p.wait("vector", mm)
            p.wait("vector", st)
            p.run("vector", lambda e, bank=bank, gn=gn:
                  e.tensor_tensor(xf[:, 0:gn], psO[bank][:, 0:gn], xin[:, 0:gn], ALU.add))
            psfree[bank] = p.mark("vector")
            p.run("vector", lambda e, gn=gn: e.tensor_copy(xb[:, 0:gn], xf[:, 0:gn]))
            p.wait_mark("scalar", p.mark("vector"))
            p.wait("scalar", st)
            p.dma("scalar", d["xmid_f"][dc, :, g0:g0 + gn], xf[:, 0:gn], st)
            p.dma("scalar", d["xmid_b"][dc, :, g0:g0 + gn], xb[:, 0:gn], st)
    p.wait("sync", st)
    p.wait("scalar", st)
    p.barrier()
    p.free_tmps()


def emit_p2_mask_col(p, dstn, idx_extra, col):
    d = p.dram
    ld = p.sem("p2m_ld"); st = p.sem("p2m_st")
    kc = p.sbuf("p2m_kc", [128, 1], F32)
    tcol = p.sbuf("p2m_tcol", [128, 2], F32)
    tcolb = p.sbuf("p2m_tcolb", [128, 1], BF16)
    p.dma("sync", kc[:], d["keep_col"], ld)
    for c in range(NCH):
        p.wait("sync", st)
        p.dma("sync", tcol[:, 0:1], d[dstn + "_f"][idx_extra + (c, slice(0, 128), slice(col, col + 1))], ld,
              allow_slow_non_contiguous=True)
        p.wait("vector", ld)
        p.run("vector", lambda e: e.tensor_scalar(tcol[:, 0:1], tcol[:, 0:1],
                                                  kc[:, 0:1], None, ALU.mult))
        p.run("vector", lambda e: e.tensor_copy(tcolb[:], tcol[:, 0:1]))
        p.wait_mark("scalar", p.mark("vector"))
        p.dma("scalar", d[dstn + "_f"][idx_extra + (c, slice(0, 128), slice(col, col + 1))], tcol[:, 0:1], st,
              allow_slow_non_contiguous=True)
        p.dma("scalar", d[dstn + "_b"][idx_extra + (c, slice(0, 128), slice(col, col + 1))], tcolb[:], st,
              allow_slow_non_contiguous=True)
    p.wait("scalar", st)
    p.barrier()


def emit_p2_stageC2(p, li):
    d = p.dram
    ld = p.sem(f"p2c2{li}_ld"); mm = p.sem(f"p2c2{li}_mm"); st = p.sem(f"p2c2{li}_st")
    wld = p.sem(f"p2c2{li}_wld")
    NHC = HIDP // 128
    w1a = p.sbuf(f"p2c2{li}_w1a", [128, 2 * NCH * 128], BF16, tmp=True)
    w1g = p.sbuf(f"p2c2{li}_w1g", [128, NCH * HIDP], BF16, tmp=True)
    w2 = p.sbuf(f"p2c2{li}_w2", [128, NHC * 1024], BF16, tmp=True)
    w1aT = d[f"f{li}_w1a"].rearrange("(c p) h -> p c h", p=128)
    for c in range(NCH):
        p.dma("gpsimd", w1g[:, c * HIDP:(c + 1) * HIDP],
              d[f"f{li}_w1g"][c * 128:(c + 1) * 128, :], wld)
    for j in range(NHC):
        p.dma("gpsimd", w2[:, j * 1024:(j + 1) * 1024],
              d[f"f{li}_w2"][j * 128:(j + 1) * 128, :], wld)
    b1a = p.sbuf(f"p2c2{li}_b1a", [128, NHC], F32, tmp=True)
    b1g = p.sbuf(f"p2c2{li}_b1g", [128, NHC], F32, tmp=True)
    lng = p.sbuf(f"p2c2{li}_lng", [128, NCH], F32, tmp=True)
    lnb = p.sbuf(f"p2c2{li}_lnb", [128, NCH], F32, tmp=True)
    b2c = p.sbuf(f"p2c2{li}_b2", [128, NCH], F32, tmp=True)
    p.dma("sync", b1a[:], d[f"f{li}_b1a"].rearrange("c p -> p c"), wld,
          allow_slow_non_contiguous=True)
    p.dma("sync", b1g[:], d[f"f{li}_b1g"].rearrange("c p -> p c"), wld,
          allow_slow_non_contiguous=True)
    p.dma("sync", lng[:], d[f"f{li}_lng"].rearrange("c p -> p c"), wld,
          allow_slow_non_contiguous=True)
    p.dma("sync", lnb[:], d[f"f{li}_lnb"].rearrange("c p -> p c"), wld,
          allow_slow_non_contiguous=True)
    p.dma("sync", b2c[:], d[f"f{li}_b2"].rearrange("c p -> p c"), wld,
          allow_slow_non_contiguous=True)
    onescol_b = p.sbuf("p2_onescol", [128, 1], BF16)
    onesr32 = p.sb["ones_1x128"]
    p.run("vector", lambda e: e.memset(onescol_b[:], 1.0))
    onescol32 = p.sbuf("p2_onescol32", [128, 1], F32)
    p.run("vector", lambda e: e.memset(onescol32[:], 1.0))

    xw = p.sbuf(f"p2c2_xw", [128, NCH * 520], F32, tmp=True)
    xh = p.sbuf(f"p2c2_xh", [128, NCH * 512], BF16, tmp=True)
    xsq = p.sbuf(f"p2c2_xsq", [128, 512], F32, tmp=True)
    rows = p.sbuf(f"p2c2_rows", [1, 4 * 512], F32, tmp=True)
    glu = p.sbuf(f"p2c2_glu", [128, NHC * 512], BF16, tmp=True)
    gg = p.sbuf(f"p2c2_gg", [128, 512], BF16, tmp=True)
    xof = p.sbuf(f"p2c2_xof", [128, 512], F32, tmp=True)
    xob = p.sbuf(f"p2c2_xob", [128, 512], BF16, tmp=True)
    xmf = p.sbuf(f"p2c2_xmf", [128, 512], F32, tmp=True)
    psStat = [p.PS(0)[0:1, :], p.PS(1)[0:1, :]]
    psB = [p.PS(2), p.PS(3)]
    psAG = [p.PS(4), p.PS(5)]
    psO = [p.PS(0), p.PS(1)]
    p.wait("tensor", wld)
    p.wait_mark("tensor", p.mark("vector"))
    ps_o_free = [None, None]
    for (g0, gn) in _groups(P2_RNG[li]["C2"]):
        p.wait_mark("sync", p.mark("tensor"))
        if ps_o_free[0] is not None:
            p.wait_mark("tensor", ps_o_free[0])
            p.wait_mark("tensor", ps_o_free[1])
        for c in range(NCH):
            p.dma("sync", xw[:, c * 520:c * 520 + 8 + gn],
                  d["xmid_f"][c, :, g0 - 8:g0 + gn], ld)
        p.wait("tensor", ld)
        hview = lambda c, gn=gn: xw[:, c * 520 + (8 if c < 4 else 7): c * 520 + (8 if c < 4 else 7) + gn]
        xwb = p.sbuf(f"p2c2_xwb", [128, NCH * 512], BF16, tmp=True)
        p.wait("vector", ld)
        for c in range(NCH):
            p.run("vector", lambda e, c=c, gn=gn, hv=hview:
                  e.tensor_copy(xwb[:, c * 512:c * 512 + gn], hv(c)))
        p.wait_mark("tensor", p.mark("vector"))
        for c in range(NCH):
            p.run("tensor", lambda e, c=c, gn=gn:
                  e.matmul(psStat[0][0:1, 0:gn], onescol_b[:],
                           xwb[:, c * 512:c * 512 + gn],
                           start=(c == 0), stop=(c == NCH - 1)),
                  mm if c == NCH - 1 else None)
        sq_mark = None
        p.wait("scalar", mm)
        for c in range(NCH):
            p.wait_mark("scalar", p.mark("vector") if c == 0 else sq_mark)
            p.run("scalar", lambda e, c=c, gn=gn, hv=hview:
                  e.activation(xsq[:, 0:gn], hv(c), AF.Square))
            sq_mark = p.mark("scalar")
            p.wait_mark("vector", sq_mark)
            p.run("vector", lambda e, c=c, gn=gn:
                  e.tensor_copy(xwb[:, c * 512:c * 512 + gn], xsq[:, 0:gn]))
            sq_mark = p.mark("vector")
        p.wait_mark("tensor", sq_mark)
        p.wait("tensor", mm)
        for c in range(NCH):
            p.run("tensor", lambda e, c=c, gn=gn:
                  e.matmul(psStat[1][0:1, 0:gn], onescol_b[:],
                           xwb[:, c * 512:c * 512 + gn],
                           start=(c == 0), stop=(c == NCH - 1)),
                  mm if c == NCH - 1 else None)
        p.wait("vector", mm)
        mu = rows[0:1, 0:512]; var = rows[0:1, 512:1024]
        rstd = rows[0:1, 1024:1536]; tmpr = rows[0:1, 1536:2048]
        p.run("vector", lambda e, gn=gn: e.tensor_scalar(mu[0:1, 0:gn], psStat[0][0:1, 0:gn], 1.0 / D, None, ALU.mult))
        p.run("vector", lambda e, gn=gn: e.tensor_scalar(var[0:1, 0:gn], psStat[1][0:1, 0:gn], 1.0 / D, None, ALU.mult))
        p.run("vector", lambda e, gn=gn: e.tensor_tensor(tmpr[0:1, 0:gn], mu[0:1, 0:gn], mu[0:1, 0:gn], ALU.mult))
        p.run("vector", lambda e, gn=gn: e.tensor_tensor(var[0:1, 0:gn], var[0:1, 0:gn], tmpr[0:1, 0:gn], ALU.subtract))
        p.run("vector", lambda e, gn=gn: e.tensor_scalar_add(var[0:1, 0:gn], var[0:1, 0:gn], 1e-5))
        p.run("vector", lambda e, gn=gn: e.reciprocal(rstd[0:1, 0:gn], var[0:1, 0:gn]))
        p.wait_mark("scalar", p.mark("vector"))
        p.run("scalar", lambda e, gn=gn: e.sqrt(rstd[0:1, 0:gn], rstd[0:1, 0:gn]))
        p.wait_mark("tensor", p.mark("scalar"))
        p.run("tensor", lambda e, gn=gn: e.matmul(psB[0][:, 0:gn], onesr32[:], mu[0:1, 0:gn], start=True, stop=True))
        p.run("tensor", lambda e, gn=gn: e.matmul(psB[1][:, 0:gn], onesr32[:], rstd[0:1, 0:gn], start=True, stop=True), mm)
        p.wait("vector", mm)
        for c in range(NCH):
            p.run("vector", lambda e, c=c, gn=gn, hv=hview:
                  e.tensor_tensor(xsq[:, 0:gn], hv(c), psB[0][:, 0:gn], ALU.subtract))
            p.run("vector", lambda e, c=c, gn=gn:
                  e.tensor_tensor(xsq[:, 0:gn], xsq[:, 0:gn], psB[1][:, 0:gn], ALU.mult))
            p.run("vector", lambda e, c=c, gn=gn:
                  e.tensor_scalar(xh[:, c * 512:c * 512 + gn], xsq[:, 0:gn],
                                  lng[:, c:c + 1], lnb[:, c:c + 1], ALU.mult, ALU.add))
        xh_done = p.mark("vector")
        p.wait_mark("tensor", xh_done)
        w1a_busy = [None, None]
        for j in range(NHC):
            jb = j % 2
            if w1a_busy[jb] is not None:
                p.wait_mark("gpsimd", w1a_busy[jb])
            p.dma("gpsimd",
                  w1a[:, jb * 1024:(jb + 1) * 1024].rearrange("p (c h) -> p c h", h=128),
                  w1aT[:, :, j * 128:(j + 1) * 128], ld)
            p.wait("tensor", ld)
            for c in range(NCH):
                p.run("tensor", lambda e, j=j, jb=jb, c=c, gn=gn:
                      e.matmul(psAG[0][:, 0:gn],
                               w1a[:, jb * 1024 + c * 128:jb * 1024 + (c + 1) * 128],
                               xh[:, c * 512:c * 512 + gn],
                               start=(c == 0), stop=(c == NCH - 1)))
            for c in range(NCH):
                p.run("tensor", lambda e, j=j, c=c, gn=gn:
                      e.matmul(psAG[1][:, 0:gn],
                               w1g[:, c * HIDP + j * 128:c * HIDP + (j + 1) * 128],
                               xh[:, c * 512:c * 512 + gn],
                               start=(c == 0), stop=(c == NCH - 1)),
                      mm if c == NCH - 1 else None)
            p.wait("scalar", mm)
            p.run("scalar", lambda e, j=j, gn=gn:
                  e.activation(gg[:, 0:gn], psAG[1][:, 0:gn], AF.Gelu,
                               bias=b1g[:, j:j + 1], scale=1.0))
            p.wait_mark("vector", p.mark("scalar"))
            p.run("vector", lambda e, j=j, gn=gn:
                  e.scalar_tensor_tensor(glu[:, j * 512:j * 512 + gn],
                                         psAG[0][:, 0:gn], b1a[:, j:j + 1],
                                         gg[:, 0:gn], ALU.add, ALU.mult))
            w1a_busy[jb] = p.mark("tensor")
            p.wait_mark("tensor", p.mark("vector"))
        glu_done = p.mark("vector")
        p.wait_mark("tensor", glu_done)
        for dc in range(NCH):
            bank = dc % 2
            if ps_o_free[bank] is not None:
                p.wait_mark("tensor", ps_o_free[bank])
            for j in range(NHC):
                p.run("tensor", lambda e, j=j, dc=dc, bank=bank, gn=gn:
                      e.matmul(psO[bank][:, 0:gn],
                               w2[:, j * 1024 + dc * 128:j * 1024 + (dc + 1) * 128],
                               glu[:, j * 512:j * 512 + gn],
                               start=(j == 0), stop=(j == NHC - 1)),
                      mm if j == NHC - 1 else None)
            p.wait_mark("gpsimd", p.mark("vector"))
            p.dma("gpsimd", xmf[:, 0:gn], d["xmid_f"][dc, :, g0:g0 + gn], ld)
            p.wait("vector", ld)
            p.wait("vector", mm)
            p.wait("vector", st)
            p.run("vector", lambda e, dc=dc, bank=bank, gn=gn:
                  e.tensor_scalar(xof[:, 0:gn], psO[bank][:, 0:gn],
                                  b2c[:, dc:dc + 1], None, ALU.add))
            p.run("vector", lambda e, gn=gn:
                  e.tensor_tensor(xof[:, 0:gn], xof[:, 0:gn], xmf[:, 0:gn], ALU.add))
            ps_o_free[bank] = p.mark("vector")
            p.run("vector", lambda e, gn=gn: e.tensor_copy(xob[:, 0:gn], xof[:, 0:gn]))
            p.wait_mark("scalar", p.mark("vector"))
            p.wait("scalar", st)
            p.dma("scalar", d["xout_f"][li, dc, :, g0:g0 + gn], xof[:, 0:gn], st)
            p.dma("scalar", d["xout_b"][li, dc, :, g0:g0 + gn], xob[:, 0:gn], st)
    p.wait("scalar", st)
    p.wait("sync", st)
    p.barrier()
    p.free_tmps()


def emit_p2(p: Prog):
    p.free_left_after("ones_1x128")
    for li in range(2):
        emit_p2_stageA(p, li)
        emit_p2_stageB(p, li)
        emit_p2_stageC1(p, li)
        emit_p2_mask_col(p, "xmid", (), OWN0 - 1)
        emit_p2_stageC2(p, li)
        if li == 0:
            emit_p2_mask_col(p, "xout", (0,), OWN0 - 1)



def emit_p3(p: Prog):
    d = p.dram
    ld = p.sem("p3_ld"); mm = p.sem("p3_mm"); st = p.sem("p3_st")
    identb = p.sb["identb"]; ident32 = p.sb["ident32"]
    xs = p.sbuf("p3_xs", [128, D], F32, tmp=True)
    ao = p.sbuf("p3_ao", [128, D], F32, tmp=True)
    ff = p.sbuf("p3_ff", [128, D], F32, tmp=True)
    x1t = p.sbuf("p3_x1t", [128, D], F32, tmp=True)
    xnt = p.sbuf("p3_xnt", [128, D], F32, tmp=True)
    stat = p.sbuf("p3_stat", [128, 4], F32, tmp=True)
    junk = p.sbuf("p3_junk", [128, D], F32, tmp=True)
    evf = p.sbuf("p3_evf", [128, D], F32, tmp=True)
    evb = p.sbuf("p3_evb", [128, D], BF16, tmp=True)
    evx = p.sbuf("p3_evx", [128, D], F32, tmp=True)
    ffs = p.sbuf("p3_ffs", [128, D], F32, tmp=True)
    pst32 = p.PS(0)[:, 0:128]
    mu, nmu, var, rstd = (stat[:, i:i + 1] for i in range(4))
    own_from = p.dram["x_b"]
    for i in range(T // 128):
        r0 = i * 128
        p.wait_mark("sync", p.mark("tensor"))
        p.dma("sync", xs[:], d["x_loc"][HALO + r0:HALO + r0 + 128, :], ld)
        p.dma("sync", ao[:], d["attnout"][r0:r0 + 128, :], ld)
        for c in range(NCH):
            p.dma("sync", ffs[:, c * 128:(c + 1) * 128],
                  d["xout_f"][1, c, :, OWN0 + r0:OWN0 + r0 + 128], ld)
        p.wait("tensor", ld)
        for c in range(NCH):
            p.run("tensor", lambda e, c=c:
                  e.transpose(pst32, ffs[:, c * 128:(c + 1) * 128], ident32[:]), mm)
            p.wait("vector", mm)
            p.run("vector", lambda e, c=c: e.tensor_copy(ff[:, c * 128:(c + 1) * 128], pst32))
            p.wait_mark("tensor", p.mark("vector"))
        p.wait("vector", ld)
        p.run("vector", lambda e: e.tensor_add(x1t[:], xs[:], ao[:]))
        p.run("vector", lambda e: e.tensor_add(x1t[:], x1t[:], ff[:]))
        p.run("vector", lambda e: e.reduce_sum(mu, x1t[:], AX.X))
        p.run("vector", lambda e: e.tensor_scalar_mul(mu, mu, 1.0 / D))
        p.run("vector", lambda e: e.tensor_scalar(nmu, mu, -1.0, None, ALU.mult))
        p.wait_mark("scalar", p.mark("vector"))
        p.run("scalar", lambda e: e.activation(junk[:], x1t[:], AF.Square,
                                               bias=nmu, scale=1.0, accum_out=var))
        p.wait_mark("vector", p.mark("scalar"))
        p.run("vector", lambda e: e.tensor_scalar(var, var, 1.0 / D, 1e-5, ALU.mult, ALU.add))
        p.run("vector", lambda e: e.reciprocal(rstd, var))
        p.wait_mark("scalar", p.mark("vector"))
        p.run("scalar", lambda e: e.sqrt(rstd, rstd))
        p.run("scalar", lambda e: e.activation(xnt[:], x1t[:], AF.Identity, bias=nmu, scale=1.0))
        p.run("scalar", lambda e: e.mul(xnt[:], xnt[:], rstd))
        p.wait_mark("tensor", p.mark("scalar"))
        for c in range(NCH):
            p.run("tensor", lambda e, c=c: e.transpose(pst32, x1t[:, c * 128:(c + 1) * 128], ident32[:]), mm)
            p.wait("vector", mm)
            p.wait("vector", st)
            p.run("vector", lambda e, c=c: e.tensor_copy(evx[:, c * 128:(c + 1) * 128], pst32))
            p.wait_mark("tensor", p.mark("vector"))
            p.run("tensor", lambda e, c=c: e.transpose(pst32, xnt[:, c * 128:(c + 1) * 128], ident32[:]), mm)
            p.wait("vector", mm)
            p.run("vector", lambda e, c=c: e.tensor_copy(evf[:, c * 128:(c + 1) * 128], pst32))
            p.run("vector", lambda e, c=c: e.tensor_copy(evb[:, c * 128:(c + 1) * 128],
                                                         evf[:, c * 128:(c + 1) * 128]))
            p.wait_mark("tensor", p.mark("vector"))
        p.wait_mark("scalar", p.mark("vector"))
        for c in range(NCH):
            p.dma("scalar", d["x1f"][c, :, r0:r0 + 128], evx[:, c * 128:(c + 1) * 128], st)
            p.dma("scalar", d["xn1f"][c, :, r0:r0 + 128], evf[:, c * 128:(c + 1) * 128], st)
            p.dma("scalar", d["xn1b"][c, :, r0:r0 + 128], evb[:, c * 128:(c + 1) * 128], st)
    p.wait("scalar", st)
    p.wait("sync", st)
    p.barrier()
    p.free_tmps()





def emit_p5a(p: Prog):
    d = p.dram
    ld = p.sem("p5a_ld"); mm = p.sem("p5a_mm"); st = p.sem("p5a_st")
    wld = p.sem("p5a_wld")
    win = p.sbuf("p5_win", [128, NCH * 128], F32, tmp=True)
    wa = p.sbuf("p5_wa", [128, NCH * 128], F32, tmp=True)
    winb = p.sbuf("p5_winb", [128, NCH * 128], BF16, tmp=True)
    wab = p.sbuf("p5_wab", [128, NCH * 128], BF16, tmp=True)
    bb = p.sbuf("p5_bb", [128, 4 * NCH], F32, tmp=True)
    for c in range(NCH):
        p.dma("sync", win[:, c * 128:(c + 1) * 128], d["g_win"][c], wld)
        p.dma("sync", wa[:, c * 128:(c + 1) * 128], d["g_wa"][c], wld)
    p.dma("sync", bb[:, 0:NCH], d["g_bin"].rearrange("c p -> p c"), wld,
          allow_slow_non_contiguous=True)
    p.dma("sync", bb[:, NCH:2 * NCH], d["g_ba"].rearrange("c p -> p c"), wld,
          allow_slow_non_contiguous=True)
    p.dma("sync", bb[:, 2 * NCH:3 * NCH], d["g_aparam"].rearrange("c p -> p c"), wld,
          allow_slow_non_contiguous=True)
    p.wait("vector", wld)
    p.run("vector", lambda e: e.tensor_copy(winb[:], win[:]))
    p.run("vector", lambda e: e.tensor_copy(wab[:], wa[:]))
    ap = bb[:, 2 * NCH:3 * NCH]; spc = bb[:, 3 * NCH:4 * NCH]
    t1 = p.sbuf("p5_t1", [128, NCH], F32, tmp=True)
    p.wait_mark("scalar", p.mark("vector"))
    p.run("scalar", lambda e: e.activation(t1[:], ap, AF.Abs))
    p.run("scalar", lambda e: e.activation(t1[:], t1[:], AF.Exp, scale=-1.0))
    p.run("scalar", lambda e: e.activation(t1[:], t1[:], AF.Ln, bias=1.0, scale=1.0))
    p.wait_mark("vector", p.mark("scalar"))
    p.run("vector", lambda e: e.tensor_scalar_max(spc[:], ap, 0.0))
    p.run("vector", lambda e: e.tensor_add(spc[:], spc[:], t1[:]))
    p.run("vector", lambda e: e.tensor_scalar(spc[:], spc[:], 1e-3, -1.0, ALU.add, ALU.mult))
    wb_done = p.mark("vector")

    xnb = p.sbuf("p5_xnb", [128, T], BF16, tmp=True)
    xnf = p.sbuf("p5_xnf", [128, T], F32, tmp=True)
    gx = p.sbuf("p5_gx", [128, T], F32, tmp=True)
    at = p.sbuf("p5_at", [128, T + 1], F32, tmp=True)
    pp = p.sbuf("p5_pp", [128, T + 1], F32, tmp=True)
    wv = p.sbuf("p5_wv", [128, T], F32, tmp=True)
    nt = p.sbuf("p5_nt", [128, T], F32, tmp=True)
    carry = p.sbuf("p5_carry", [128, 32], F32, tmp=True)
    psg = [p.PS(0), p.PS(1)]
    p.wait_mark("tensor", wb_done)
    for c in range(NCH):
        p.wait_mark("sync", p.mark("tensor"))
        p.wait_mark("sync", p.mark("vector"))
        p.dma("sync", xnb[:], d["xn1b"][c], ld)
        p.dma("sync", xnf[:], d["xn1f"][c], ld)
        if DEBUG_P5 and c == 0:
            p.wait("gpsimd", ld)
            p.dma("gpsimd", d["of_l"][256:384, 1024:1024 + T], xnf[:], p.sem("p5dbg"))
            p.wait("gpsimd", p.sem("p5dbg"))
        p.wait("tensor", ld)
        p.wait("vector", st)
        for which, wsb, bcol in ((0, winb, 0), (1, wab, 1)):
            for t8 in range(T // 512):
                bank = t8 % 2
                if t8 >= 2 or which == 1:
                    p.wait_mark("tensor", gev[bank])
                p.run("tensor", lambda e, c=c, wsb=wsb, t8=t8, bank=bank:
                      e.matmul(psg[bank][:], wsb[:, c * 128:(c + 1) * 128],
                               xnb[:, t8 * 512:(t8 + 1) * 512],
                               start=True, stop=True), mm)
                p.wait("scalar", mm)
                dstt = gx if which == 0 else at
                off = 0 if which == 0 else 1
                p.run("scalar", lambda e, c=c, t8=t8, bank=bank, dstt=dstt, off=off, bcol=bcol:
                      e.activation(dstt[:, off + t8 * 512: off + (t8 + 1) * 512],
                                   psg[bank][:], AF.Sigmoid,
                                   bias=bb[:, bcol * NCH + c:bcol * NCH + c + 1], scale=1.0))
                if t8 == 0 and which == 0:
                    gev = [None, None]
                gev[bank] = p.mark("scalar")
        if DEBUG_P5:
            p.wait_mark("scalar", p.mark("scalar"))
            p.dma("scalar", d["moeacc"][c, :, 0:T], gx[:], p.sem("p5dbg"))
            p.wait("scalar", p.sem("p5dbg"))
        p.wait_mark("vector", p.mark("scalar"))
        p.run("vector", lambda e, c=c:
              e.tensor_scalar(at[:, 1:T + 1], at[:, 1:T + 1],
                              spc[:, c:c + 1], None, ALU.mult))
        p.wait_mark("scalar", p.mark("vector"))
        p.run("scalar", lambda e: e.activation(at[:, 1:T + 1], at[:, 1:T + 1], AF.Exp))
        p.wait_mark("vector", p.mark("scalar"))
        p.run("vector", lambda e: e.tensor_scalar_min(at[:, 1:T + 1], at[:, 1:T + 1], 1.0 - 1e-5))
        p.run("vector", lambda e: e.memset(at[:, 0:1], 1.0))
        p.fence("vector")
        p.run("vector", lambda e: e.tensor_tensor_scan(
            pp[:], at[:, 0:T + 1], at[:, 0:T + 1], 1.0, ALU.mult, ALU.bypass))
        p.fence("vector")
        p.run("vector", lambda e: e.tensor_tensor(nt[:], at[:, 1:T + 1], at[:, 1:T + 1], ALU.mult))
        p.run("vector", lambda e: e.tensor_scalar(nt[:], nt[:], -1.0, 1.0, ALU.mult, ALU.add))
        p.run("vector", lambda e: e.tensor_scalar_max(nt[:], nt[:], 1e-5))
        p.wait_mark("scalar", p.mark("vector"))
        p.run("scalar", lambda e: e.sqrt(nt[:], nt[:]))
        if DEBUG_P5 and c == 0:
            p.wait_mark("gpsimd", p.mark("scalar"))
            p.dma("gpsimd", d["of_l"][0:128, 0:T], nt[:], p.sem("p5dbg"))
            p.wait("gpsimd", p.sem("p5dbg"))
        p.wait_mark("vector", p.mark("scalar"))
        if DEBUG_P5 and c == 0:
            p.wait_mark("gpsimd", p.mark("vector"))
            p.dma("gpsimd", d["of_l"][256:384, 0:T], xnf[:], p.sem("p5dbg"))
            p.dma("gpsimd", d["of_l"][384:512, 0:T], gx[:], p.sem("p5dbg"))
            p.wait("gpsimd", p.sem("p5dbg"))
            p.wait("vector", p.sem("p5dbg"))
        p.run("vector", lambda e: e.tensor_mul(gx[:], gx[:], xnf[:]))
        if DEBUG_P5 and c == 0:
            p.wait_mark("gpsimd", p.mark("vector"))
            p.dma("gpsimd", d["of_l"][128:256, 0:T], gx[:], p.sem("p5dbg"))
            p.wait("gpsimd", p.sem("p5dbg"))
        p.run("vector", lambda e: e.tensor_mul(nt[:], nt[:], gx[:]))
        p.run("vector", lambda e: e.tensor_scalar_max(wv[:], pp[:, 0:T], 1e-7))
        p.fence("vector")
        p.run("vector", lambda e: e.tensor_scalar_add(gx[:], nt[:], 1e-7))
        p.fence("vector")
        p.run("vector", lambda e: e.reciprocal(wv[:], wv[:]))
        p.fence("vector")
        p.run("vector", lambda e: e.tensor_tensor(wv[:], gx[:], wv[:], ALU.mult))
        p.fence("vector")
        p.run("vector", lambda e: e.tensor_tensor_scan(
            gx[:], wv[:], wv[:], 0.0, ALU.add, ALU.bypass))
        p.fence("vector")
        p.run("vector", lambda e, c=c: e.tensor_copy(carry[:, c:c + 1], pp[:, T:T + 1]))
        p.run("vector", lambda e, c=c: e.tensor_copy(carry[:, 16 + c:17 + c], gx[:, T - 1:T]))
        p.wait_mark("scalar", p.mark("vector"))
        p.wait("scalar", st)
        p.dma("scalar", d["ppl"][c], pp[:], st)
        p.dma("scalar", d["normed"][c], nt[:], st)
    p.wait_mark("scalar", p.mark("vector"))
    p.dma("scalar", d["carry_out"][0].rearrange("(p c) -> p c", p=128), carry[:], st)
    p.wait("scalar", st)
    p.barrier()
    p.free_tmps()


def emit_p5b(p: Prog):
    d = p.dram
    ld = p.sem("p5b_ld"); st = p.sem("p5b_st")
    pp = p.sbuf("p5b_pp", [128, T + 1], F32, tmp=True)
    nt = p.sbuf("p5b_nt", [128, T], F32, tmp=True)
    wv = p.sbuf("p5b_wv", [128, T], F32, tmp=True)
    hh = p.sbuf("p5b_h", [128, T], F32, tmp=True)
    cu = p.sbuf("p5b_cu", [128, 1], F32, tmp=True)
    ginit = p.sbuf("p5b_gi", [128, 32], F32, tmp=True)
    initc = p.sbuf("p5b_init", [128, 4], F32, tmp=True)
    p.dma("sync", cu[:], d["carry_use"], ld)
    p.dma("sync", ginit[:], d["carry_gath"][0].rearrange("(p c) -> p c", p=128), ld)
    for c in range(NCH):
        p.wait_mark("sync", p.mark("vector"))
        p.dma("sync", pp[:], d["ppl"][c], ld)
        p.dma("sync", nt[:], d["normed"][c], ld)
        p.wait("vector", ld)
        p.wait("vector", st)
        p.run("vector", lambda e, c=c:
              e.tensor_scalar(initc[:, 0:1], ginit[:, c:c + 1], cu[:, 0:1], None, ALU.mult))
        p.run("vector", lambda e:
              e.tensor_scalar(initc[:, 1:2], cu[:, 0:1], -1.0, 1.0, ALU.mult, ALU.add))
        p.run("vector", lambda e:
              e.tensor_add(initc[:, 0:1], initc[:, 0:1], initc[:, 1:2]))
        p.run("vector", lambda e, c=c:
              e.tensor_scalar(initc[:, 2:3], ginit[:, 16 + c:17 + c], cu[:, 0:1], None, ALU.mult))
        p.run("vector", lambda e:
              e.tensor_scalar(pp[:, 0:T], pp[:, 0:T], initc[:, 0:1], None, ALU.mult))
        p.run("vector", lambda e: e.tensor_scalar_max(pp[:, 0:T], pp[:, 0:T], 1e-7))
        p.run("vector", lambda e: e.tensor_scalar_add(wv[:], nt[:], 1e-7))
        p.fence("vector")
        p.run("vector", lambda e: e.reciprocal(hh[:], pp[:, 0:T]))
        p.fence("vector")
        p.run("vector", lambda e: e.tensor_tensor(wv[:], wv[:], hh[:], ALU.mult))
        p.fence("vector")
        p.run("vector", lambda e: e.tensor_tensor_scan(
            hh[:], wv[:], wv[:], initc[:, 2:3], ALU.add, ALU.bypass))
        p.fence("vector")
        p.run("vector", lambda e: e.tensor_tensor(hh[:], hh[:], pp[:, 0:T], ALU.mult))
        p.wait_mark("scalar", p.mark("vector"))
        p.wait("scalar", st)
        p.dma("scalar", d["hscan"][c], hh[:], st)
    p.wait("scalar", st)
    p.barrier()
    p.free_tmps()



def emit_p6a(p: Prog):
    d = p.dram
    ld = p.sem("p6a_ld"); mm = p.sem("p6a_mm"); st = p.sem("p6a_st")
    ident32 = p.sb["ident32"]
    rw = p.sbuf("p6a_rw", [128, NCH * 8], BF16, tmp=True)
    eb = p.sbuf("p6a_eb", [128, 8], F32, tmp=True)
    io8 = p.sbuf("p6a_io8", [128, 8], F32, tmp=True)
    xnb = p.sbuf("p6a_xnb", [128, NCH * 128], BF16, tmp=True)
    lg = p.sbuf("p6a_lg", [128, 8], F32, tmp=True)
    r1 = p.sbuf("p6a_r1", [128, 8], F32, tmp=True)
    m1 = p.sbuf("p6a_m1", [128, 8], U8, tmp=True)
    m2 = p.sbuf("p6a_m2", [128, 8], U8, tmp=True)
    big = p.sbuf("p6a_big", [128, 8], F32, tmp=True)
    cw = p.sbuf("p6a_cw", [128, 8], F32, tmp=True)
    cwt = p.sbuf("p6a_cwt", [8, 128], F32, tmp=True)
    psr = p.PS(0)[:, 0:8]
    pse = p.PS(1)[:, 0:8]
    pst = p.PS(2)[0:8, 0:128]
    onesr32 = p.sb["ones_1x128"]
    for c in range(NCH):
        p.dma("gpsimd", rw[:, c * 8:(c + 1) * 8], d["m_routerT"][c * 128:(c + 1) * 128, :],
              p.sem("p6a_wld"))
    p.dma("sync", io8[:], d["iota8"], ld)
    ebr = p.sbuf("p6a_ebr", [1, 8], F32, tmp=True)
    p.dma("sync", ebr[:], d["m_ebias"], ld)
    p.run("vector", lambda e: e.memset(big[:], 1e9))
    p.wait("tensor", ld)
    p.wait("tensor", p.sem("p6a_wld"))
    p.run("tensor", lambda e: e.matmul(pse[0:128, 0:8], onesr32[:], ebr[:],
                                       start=True, stop=True), mm)
    p.wait("vector", mm)
    p.run("vector", lambda e: e.tensor_copy(eb[:], pse[0:128, 0:8]))
    for i in range(T // 128):
        r0 = i * 128
        p.wait_mark("sync", p.mark("tensor"))
        for c in range(NCH):
            p.dma("sync", xnb[:, c * 128:(c + 1) * 128], d["xn1b"][c, :, r0:r0 + 128], ld)
        p.wait("tensor", ld)
        p.wait_mark("tensor", p.mark("vector"))
        for c in range(NCH):
            p.run("tensor", lambda e, c=c:
                  e.matmul(psr, xnb[:, c * 128:(c + 1) * 128], rw[:, c * 8:(c + 1) * 8],
                           start=(c == 0), stop=(c == NCH - 1)),
                  mm if c == NCH - 1 else None)
        p.wait("vector", mm)
        p.run("vector", lambda e: e.tensor_tensor(lg[:], psr, eb[:], ALU.add))
        p.run("vector", lambda e: e.reduce_max(r1[:, 0:1], lg[:], AX.X))
        p.run("vector", lambda e: e.tensor_scalar(m1[:], lg[:], r1[:, 0:1], None, ALU.is_ge))
        p.run("vector", lambda e: e.select(cw[:], m1[:], io8[:], big[:]))
        p.run("vector", lambda e: e.tensor_reduce(r1[:, 1:2], cw[:], AX.X, ALU.min))
        p.run("vector", lambda e: e.tensor_scalar(m1[:], io8[:], r1[:, 1:2], None, ALU.is_equal))
        m1fv = p.sbuf("p6a_m1f", [128, 8], F32, tmp=True)
        p.run("vector", lambda e, m1fv=m1fv: e.tensor_copy(m1fv[:], m1[:]))
        p.run("vector", lambda e, m1fv=m1fv: e.tensor_scalar(m1fv[:], m1fv[:], 2e9, None, ALU.mult))
        p.run("vector", lambda e, m1fv=m1fv: e.tensor_tensor(cw[:], lg[:], m1fv[:], ALU.subtract))
        p.run("vector", lambda e: e.reduce_max(r1[:, 2:3], cw[:], AX.X))
        p.run("vector", lambda e: e.tensor_scalar(m2[:], cw[:], r1[:, 2:3], None, ALU.is_ge))
        p.run("vector", lambda e: e.select(cw[:], m2[:], io8[:], big[:]))
        p.run("vector", lambda e: e.tensor_reduce(r1[:, 3:4], cw[:], AX.X, ALU.min))
        p.run("vector", lambda e: e.tensor_scalar(m2[:], io8[:], r1[:, 3:4], None, ALU.is_equal))
        p.run("vector", lambda e: e.tensor_tensor(r1[:, 2:3], r1[:, 2:3], r1[:, 0:1], ALU.subtract))
        p.wait_mark("scalar", p.mark("vector"))
        p.run("scalar", lambda e: e.activation(r1[:, 2:3], r1[:, 2:3], AF.Exp))
        p.wait_mark("vector", p.mark("scalar"))
        p.run("vector", lambda e: e.tensor_scalar_add(r1[:, 0:1], r1[:, 2:3], 1.0))
        p.fence("vector")
        p.run("vector", lambda e: e.reciprocal(r1[:, 0:1], r1[:, 0:1]))
        p.fence("vector")
        p.run("vector", lambda e: e.tensor_tensor(r1[:, 2:3], r1[:, 2:3], r1[:, 0:1], ALU.mult))
        p.run("vector", lambda e: e.tensor_copy(cw[:], m1[:]))
        p.run("vector", lambda e: e.tensor_scalar(cw[:], cw[:], r1[:, 0:1], None, ALU.mult))
        p.run("vector", lambda e: e.tensor_copy(lg[:], m2[:]))
        p.run("vector", lambda e: e.tensor_scalar(lg[:], lg[:], r1[:, 2:3], None, ALU.mult))
        p.run("vector", lambda e: e.tensor_add(cw[:], cw[:], lg[:]))
        p.wait_mark("tensor", p.mark("vector"))
        p.run("tensor", lambda e: e.transpose(pst, cw[:], ident32[:]), mm)
        p.wait("vector", mm)
        p.wait("vector", st)
        p.run("vector", lambda e: e.tensor_copy(cwt[:], pst))
        p.wait_mark("scalar", p.mark("vector"))
        p.dma("scalar", d["cwT"][:, r0:r0 + 128], cwt[:], st)
    p.wait("scalar", st)
    p.barrier()
    p.free_tmps()


def emit_p6(p: Prog):
    d = p.dram
    ld = p.sem("p6_ld"); mm = p.sem("p6_mm"); st = p.sem("p6_st")
    wld = p.sem("p6_wld")
    onesr32 = p.sb["ones_1x128"]
    w1 = p.sbuf("p6_w1", [128, NCH * 2048], BF16, tmp=True)
    w3 = p.sbuf("p6_w3", [128, NCH * 2048], BF16, tmp=True)
    w2 = p.sbuf("p6_w2", [128, 16 * 1024], BF16, tmp=True)
    xnb = p.sbuf("p6_xnb", [128, NCH * 1024], BF16, tmp=True)
    acc = p.sbuf("p6_acc", [128, NCH * 1024], F32, tmp=True)
    glu = p.sbuf("p6_glu", [128, 16 * 1024], BF16, tmp=True)
    sg = p.sbuf("p6_sg", [128, 512], BF16, tmp=True)
    cwb = p.sbuf("p6_cwb", [128, 1024], F32, tmp=True)
    cwr = p.sbuf("p6_cwr", [1, 1024], F32, tmp=True)
    psU = [p.PS(0), p.PS(1)]
    psG = [p.PS(2), p.PS(3)]
    psO = [p.PS(4), p.PS(5)]
    NLG = T // 1024
    guf_state = [None, None]
    of_state = [None, None]
    order = [8] + list(range(8))
    for e_i in order:
        p.wait_mark("gpsimd", p.mark("tensor"))
        for c in range(NCH):
            p.dma("gpsimd", w1[:, c * 2048:(c + 1) * 2048], d["m_w1"][e_i, c * 128:(c + 1) * 128, :], wld)
            p.dma("gpsimd", w3[:, c * 2048:(c + 1) * 2048], d["m_w3"][e_i, c * 128:(c + 1) * 128, :], wld)
        for j in range(16):
            p.dma("gpsimd", w2[:, j * 1024:(j + 1) * 1024], d["m_w2"][e_i, j * 128:(j + 1) * 128, :], wld)
        p.wait("tensor", wld)
        for lg_i in range(NLG):
            l0 = lg_i * 1024
            p.wait_mark("sync", p.mark("tensor"))
            for c in range(NCH):
                p.dma("sync", xnb[:, c * 1024:(c + 1) * 1024], d["xn1b"][c, :, l0:l0 + 1024], ld)
            if e_i != 8:
                p.dma("sync", cwr[:], d["cwT"][e_i:e_i + 1, l0:l0 + 1024], ld)
                for c in range(NCH):
                    p.dma("sync", acc[:, c * 1024:(c + 1) * 1024], d["moeacc"][c, :, l0:l0 + 1024], ld)
            p.wait("tensor", ld)
            if e_i != 8:
                for half in range(2):
                    p.run("tensor", lambda e, half=half:
                          e.matmul(psO[half][:], onesr32[:], cwr[0:1, half * 512:(half + 1) * 512],
                                   start=True, stop=True), mm)
                p.wait("vector", mm)
                for half in range(2):
                    p.run("vector", lambda e, half=half:
                          e.tensor_copy(cwb[:, half * 512:(half + 1) * 512], psO[half][:]))
                p.wait_mark("tensor", p.mark("vector"))
            guf = getattr(emit_p6, "_guf", [None, None]) if False else guf_state
            for j in range(16):
                for sub in range(2):
                    bank = sub
                    if guf[bank] is not None:
                        p.wait_mark("tensor", guf[bank])
                    for c in range(NCH):
                        p.run("tensor", lambda e, j=j, c=c, sub=sub, bank=bank:
                              e.matmul(psU[bank][:],
                                       w1[:, c * 2048 + j * 128:c * 2048 + (j + 1) * 128],
                                       xnb[:, c * 1024 + sub * 512:c * 1024 + (sub + 1) * 512],
                                       start=(c == 0), stop=(c == NCH - 1)))
                    for c in range(NCH):
                        p.run("tensor", lambda e, j=j, c=c, sub=sub, bank=bank:
                              e.matmul(psG[bank][:],
                                       w3[:, c * 2048 + j * 128:c * 2048 + (j + 1) * 128],
                                       xnb[:, c * 1024 + sub * 512:c * 1024 + (sub + 1) * 512],
                                       start=(c == 0), stop=(c == NCH - 1)),
                              mm if c == NCH - 1 else None)
                    p.wait("scalar", mm)
                    p.run("scalar", lambda e, bank=bank: e.activation(sg[:], psG[bank][:], AF.Silu))
                    p.wait_mark("vector", p.mark("scalar"))
                    p.run("vector", lambda e, j=j, sub=sub, bank=bank:
                          e.tensor_tensor(glu[:, j * 1024 + sub * 512:j * 1024 + (sub + 1) * 512],
                                          psU[bank][:], sg[:], ALU.mult))
                    if e_i != 8:
                        p.run("vector", lambda e, j=j, sub=sub:
                              e.tensor_tensor(glu[:, j * 1024 + sub * 512:j * 1024 + (sub + 1) * 512],
                                              glu[:, j * 1024 + sub * 512:j * 1024 + (sub + 1) * 512],
                                              cwb[:, sub * 512:(sub + 1) * 512], ALU.mult))
                    guf[bank] = p.mark("vector")
            gdone = p.mark("vector")
            p.wait_mark("tensor", gdone)
            of_free = of_state
            for dc in range(NCH):
                for sub in range(2):
                    bank = sub
                    if of_free[bank] is not None:
                        p.wait_mark("tensor", of_free[bank])
                    for j in range(16):
                        p.run("tensor", lambda e, j=j, dc=dc, sub=sub, bank=bank:
                              e.matmul(psO[bank][:],
                                       w2[:, j * 1024 + dc * 128:j * 1024 + (dc + 1) * 128],
                                       glu[:, j * 1024 + sub * 512:j * 1024 + (sub + 1) * 512],
                                       start=(j == 0), stop=(j == 15)),
                              mm if j == 15 else None)
                    p.wait("vector", mm)
                    if e_i == 8:
                        p.run("vector", lambda e, dc=dc, sub=sub, bank=bank:
                              e.tensor_copy(acc[:, dc * 1024 + sub * 512:dc * 1024 + (sub + 1) * 512],
                                            psO[bank][:]))
                    else:
                        p.run("vector", lambda e, dc=dc, sub=sub, bank=bank:
                              e.tensor_tensor(acc[:, dc * 1024 + sub * 512:dc * 1024 + (sub + 1) * 512],
                                              acc[:, dc * 1024 + sub * 512:dc * 1024 + (sub + 1) * 512],
                                              psO[bank][:], ALU.add))
                    of_free[bank] = p.mark("vector")
            accdone = p.mark("vector")
            p.wait_mark("scalar", accdone)
            p.wait("scalar", st)
            for c in range(NCH):
                p.dma("scalar", d["moeacc"][c, :, l0:l0 + 1024], acc[:, c * 1024:(c + 1) * 1024], st)
    p.wait("scalar", st)
    p.barrier()
    p.free_tmps()



def emit_p7(p: Prog):
    d = p.dram
    ld = p.sem("p7_ld"); mm = p.sem("p7_mm"); st = p.sem("p7_st")
    ident32 = p.sb["ident32"]
    a = p.sbuf("p7_a", [128, D], F32, tmp=True)
    b = p.sbuf("p7_b", [128, D], F32, tmp=True)
    ot = p.sbuf("p7_ot", [128, D], F32, tmp=True)
    pst32 = p.PS(0)[:, 0:128]
    for i in range(T // 128):
        r0 = i * 128
        p.wait_mark("sync", p.mark("vector"))
        for c in range(NCH):
            p.dma("sync", a[:, c * 128:(c + 1) * 128], d["x1f"][c, :, r0:r0 + 128], ld)
            p.dma("sync", b[:, c * 128:(c + 1) * 128], d["hscan"][c, :, r0:r0 + 128], ld)
        p.wait("vector", ld)
        p.run("vector", lambda e: e.tensor_add(a[:], a[:], b[:]))
        p.wait_mark("sync", p.mark("vector"))
        for c in range(NCH):
            p.dma("sync", b[:, c * 128:(c + 1) * 128], d["moeacc"][c, :, r0:r0 + 128], ld)
        p.wait("vector", ld)
        p.run("vector", lambda e: e.tensor_add(a[:], a[:], b[:]))
        p.wait_mark("tensor", p.mark("vector"))
        for c in range(NCH):
            p.run("tensor", lambda e, c=c: e.transpose(pst32, a[:, c * 128:(c + 1) * 128], ident32[:]), mm)
            p.wait("vector", mm)
            if c == 0:
                p.wait("vector", st)
            p.run("vector", lambda e, c=c: e.tensor_copy(ot[:, c * 128:(c + 1) * 128], pst32))
            p.wait_mark("tensor", p.mark("vector"))
        p.wait_mark("scalar", p.mark("vector"))
        p.dma("scalar", d["out_own"][r0:r0 + 128, :], ot[:], st)
    p.wait("scalar", st)
    p.barrier()
    p.free_tmps()
